# version 02
# baseline (speedup 1.0000x reference)
import math
from contextlib import ExitStack

import numpy as np

import concourse.bass as bass
import concourse.mybir as mybir
from concourse.ap import AP
from concourse.bass_utils import run_bass_kernel_spmd

F32 = mybir.dt.float32
F32R = mybir.dt.float32r
BF16 = mybir.dt.bfloat16
I32 = mybir.dt.int32
AF = mybir.ActivationFunctionType
ALU = mybir.AluOpType
AXX = mybir.AxisListType.X

SEQ = 2048
D = 1024
NH = 16
NT = SEQ // 128
NCH = D // 128
NP = NH // 2
NQC = SEQ // 512
EXPB = np.float32(-12.0)
F1 = np.float32(1.0)
F0 = np.float32(0.0)


class Sched:
    ENGINES = ("tensor", "vector", "scalar", "gpsimd", "sync")
    NDMA = 8

    def __init__(self):
        self.steps = []

    enabled = True

    def add(self, eng, fn, reads=(), writes=(), dma=False):
        if not self.enabled:
            return
        self.steps.append([eng, fn, tuple(reads), tuple(writes), bool(dma)])

    def barrier(self):
        self.steps.append(["BARRIER", None, (), (), False])

    def build(self, block, sems, serial=False):
        steps = self.steps
        n = len(steps)
        cpos, dpos = {}, {}
        ccount = {e: 0 for e in self.ENGINES}
        dcount = {e: 0 for e in self.ENGINES}
        deps = [set() for _ in range(n)]
        last_w, readers = {}, {}
        dqueues = {}
        bar_after = [None] * n
        snap = None
        for i, (eng, fn, rd, wr, dma) in enumerate(steps):
            if eng == "BARRIER":
                snap = (dict(ccount), dict(dcount))
                continue
            bar_after[i] = snap
            if dma:
                dpos[i] = dcount[eng]
                dcount[eng] += 1
            else:
                cpos[i] = ccount[eng]
                ccount[eng] += 1
            d = deps[i]
            for r in rd:
                j = last_w.get(r)
                if j is not None:
                    d.add(j)
            for w in wr:
                j = last_w.get(w)
                if j is not None:
                    d.add(j)
                for j in readers.get(w, ()):
                    d.add(j)
            if dma:
                q = dqueues.setdefault(eng, [])
                if len(q) >= self.NDMA:
                    d.add(q[-self.NDMA])
                q.append(i)
            for w in wr:
                last_w[w] = i
                readers[w] = []
            for r in rd:
                readers.setdefault(r, []).append(i)

        sched = self

        def mk(engine_name):
            def emit(eng):
                watermark = {}

                def wait_val(key, sem, val):
                    if watermark.get(key, -1) >= val:
                        return
                    watermark[key] = val
                    eng.wait_ge(sem, val)

                def wait_step(j):
                    e2, _, _, _, dm2 = steps[j]
                    if dm2:
                        k = dpos[j]
                        key = f"d:{e2}:{k % sched.NDMA}"
                        wait_val(key, sems[key], 16 * (k // sched.NDMA + 1))
                    elif e2 != engine_name or e2 != "tensor":
                        key = f"c:{e2}"
                        wait_val(key, sems[key], cpos[j] + 1)

                emitted_bar = None
                run_cc = {e2: 0 for e2 in sched.ENGINES}
                run_dc = {e2: 0 for e2 in sched.ENGINES}
                for i, (e, fn, rd, wr, dma) in enumerate(steps):
                    if e == "BARRIER":
                        continue
                    if serial and e != engine_name:
                        if dma:
                            run_dc[e] += 1
                        else:
                            run_cc[e] += 1
                        continue
                    if serial:
                        for e2 in sched.ENGINES:
                            if run_cc[e2] > 0:
                                wait_val(f"c:{e2}", sems[f"c:{e2}"], run_cc[e2])
                            for kk in range(max(0, run_dc[e2] - sched.NDMA), run_dc[e2]):
                                key = f"d:{e2}:{kk % sched.NDMA}"
                                wait_val(key, sems[key],
                                         16 * (kk // sched.NDMA + 1))
                        if dma:
                            run_dc[e] += 1
                        else:
                            run_cc[e] += 1
                        ins = fn(eng)
                        if dma:
                            k = dpos[i]
                            ins.then_inc(sems[f"d:{engine_name}:{k % sched.NDMA}"], 16)
                        else:
                            ins.then_inc(sems[f"c:{engine_name}"], 1)
                        continue
                    if e != engine_name:
                        continue
                    if bar_after[i] is not None and bar_after[i] is not emitted_bar:
                        cc, dc = bar_after[i]
                        emitted_bar = bar_after[i]
                        for e2 in sched.ENGINES:
                            if e2 != engine_name and cc[e2] > 0:
                                wait_val(f"c:{e2}", sems[f"c:{e2}"], cc[e2])
                            for kk in range(max(0, dc[e2] - sched.NDMA), dc[e2]):
                                key = f"d:{e2}:{kk % sched.NDMA}"
                                wait_val(key, sems[key],
                                         16 * (kk // sched.NDMA + 1))
                    for j in sorted(deps[i]):
                        wait_step(j)
                    ins = fn(eng)
                    if dma:
                        k = dpos[i]
                        ins.then_inc(sems[f"d:{engine_name}:{k % sched.NDMA}"], 16)
                    else:
                        ins.then_inc(sems[f"c:{engine_name}"], 1)
            return emit

        block.tensor(mk("tensor"))
        block.vector(mk("vector"))
        block.scalar(mk("scalar"))
        block.gpsimd(mk("gpsimd"))
        block.sync(mk("sync"))


def _oh_const():
    d = np.arange(128)
    t = np.log(np.maximum(d, 1).astype(np.float32) / np.float32(16.0))
    t = (t / np.float32(math.log(8.0))).astype(np.float32)
    t = (t * np.float32(16.0)).astype(np.float32)
    vl = np.minimum(16 + t.astype(np.int32), 31)
    bucket = np.where(d < 16, d, vl)
    oh = np.zeros((32, 128), np.float32)
    scale = np.float32((D // NH) ** (-0.5))
    for dlt in range(128):
        oh[bucket[dlt], dlt] = scale
    return oh


def build_nc(debug=False, serial=False, upto=3, skip=()):
    nc = bass.Bass("TRN2", target_bir_lowering=False)
    dt = nc.dram_tensor
    if debug:
        dbg_mod = dt("dbg_mod", [1, 6 * D], F32, kind="ExternalOutput")
        dbg_a1c = dt("dbg_a1c", [128, 16], F32, kind="ExternalOutput")
        dbg_hT = dt("dbg_hT", [128, NCH * SEQ], BF16, kind="ExternalOutput")
        dbg_E = dt("dbg_E", [128, 2 * 256], BF16, kind="ExternalOutput")
        dbg_qt = dt("dbg_qt", [128, SEQ], BF16, kind="ExternalOutput")
        dbg_kt = dt("dbg_kt", [128, SEQ], BF16, kind="ExternalOutput")
        dbg_v = dt("dbg_v", [128, NT * 130], BF16, kind="ExternalOutput")
        dbg_oT = dt("dbg_oT", [128, NCH * SEQ], BF16, kind="ExternalOutput")
        dbg_xa = dt("dbg_xa", [128, NT * D], F32, kind="ExternalOutput")
        dbg_wf = dt("dbg_wf", [128, NCH * D], BF16, kind="ExternalOutput")
        dbg_q = dt("dbg_q", [SEQ, D], BF16, kind="ExternalOutput")
        dbg_kap = dt("dbg_kap", [128, 2 * NT], F32, kind="ExternalOutput")
        dbg_y = dt("dbg_y", [SEQ, D], BF16, kind="ExternalOutput")
        dbg_x0 = dt("dbg_x0", [128, NT * D], F32, kind="ExternalOutput")
        dbg_s16 = dt("dbg_s16", [16, 128], F32, kind="ExternalOutput")
        dbg_c16 = dt("dbg_c16", [128, 16], F32, kind="ExternalOutput")
        dbg_st = dt("dbg_st", [128, 2 * NT], F32, kind="ExternalOutput")
        dbg_r = dt("dbg_r", [128, 2 * NT], F32, kind="ExternalOutput")
    x_in = dt("x", [SEQ, D], F32, kind="ExternalInput")
    wq = dt("wq", [D, D], F32, kind="ExternalInput")
    wk = dt("wk", [D, D], F32, kind="ExternalInput")
    wv = dt("wv", [D, D], F32, kind="ExternalInput")
    wo = dt("wo", [D, D], F32, kind="ExternalInput")
    wm = dt("wm", [D, 6 * D], F32, kind="ExternalInput")
    pkt = dt("pk", [D, 256], I32, kind="ExternalInput")
    ccol = dt("ccol", [128, 8], F32, kind="ExternalInput")
    bqc_d = dt("bqc", [128, 8], F32, kind="ExternalInput")
    bkc_d = dt("bkc", [128, 8], F32, kind="ExternalInput")
    bvr = dt("bvr", [1, D], F32, kind="ExternalInput")
    bor = dt("bor", [1, D], F32, kind="ExternalInput")
    fbr = dt("fbr", [1, D], F32, kind="ExternalInput")
    rel = dt("rel", [32, 16], F32, kind="ExternalInput")
    ohd = dt("oh", [32, 128], F32, kind="ExternalInput")
    wsc = dt("wsc", [1, 1], F32, kind="ExternalInput")
    idf = dt("idf", [16, 16], F32, kind="ExternalInput")
    out = dt("out", [SEQ, D], F32, kind="ExternalOutput")
    dy = dt("dy", [SEQ, D], BF16, kind="Internal")
    drows = dt("drows", [1, 8 * D], F32, kind="Internal")
    dtaus = dt("dtaus", [16, 384], BF16, kind="Internal")
    dtau = dt("dtau", [128, NH * 384], BF16, kind="Internal")

    sq = Sched()
    with ExitStack() as ctx:
        def sb(name, shape, dtp, **kw):
            return ctx.enter_context(nc.sbuf_tensor(name, shape, dtp, **kw))

        sx = sb("sx", [128, NT, D], F32)
        shT = sb("shT", [128, NCH, SEQ], BF16)
        gat = sb("gat", [128, D], BF16)
        gbiast = sb("gbiast", [128, D], BF16)
        a2t = sb("a2t", [128, D], BF16)
        b2t = sb("b2t", [128, D], BF16)
        gbw = sb("gbw", [128, D], BF16)
        sccol = sb("sccol", [128, 8], F32)
        ssil = sb("ssil", [128, 8], F32)
        scsil = sb("scsil", [128, 8], F32R)
        sbqc = sb("sbqc", [128, 8], F32)
        sbkc = sb("sbkc", [128, 8], F32)
        a1c = sb("a1c", [128, 16], F32)
        s_swb = sb("s_swb", [128, 1], F32)
        sbvb = sb("sbvb", [1, D], BF16)
        sbob = sb("sbob", [1, D], BF16)
        swscb = sb("swscb", [1, 1], BF16)
        ones1 = sb("ones1", [1, 128], BF16)
        stats6 = sb("stats6", [128, 2, 6], F32)
        stats2 = sb("stats2", [128, 2], F32)
        srcol = sb("srcol", [128, NT], F32)
        snmr = sb("snmr", [128, NT], F32)
        skap = sb("skap", [128, NT], F32)
        ssc = sb("ssc", [128, NT], F32)
        stmp1 = sb("stmp1", [128, 1], F32)
        stmp1b = sb("stmp1b", [128, 1], F32)
        stmp = sb("stmp", [128, 512], F32)
        scoll = sb("scoll", [128, 2 * NT], F32)

        sems = {}
        for e in Sched.ENGINES:
            sems[f"c:{e}"] = ctx.enter_context(nc.semaphore(f"c_{e}"))
        for e in ("sync", "gpsimd"):
            for k in range(Sched.NDMA):
                sems[f"d:{e}:{k}"] = ctx.enter_context(nc.semaphore(f"d_{e}_{k}"))

        XT = [f"x:{t}" for t in range(NT)]

        dbg_stats = [False]

        def ln_tile(t, sytile, sytag):
            sq.add("vector", lambda e, t=t: e.bn_stats(stats6[:, 0, :], sx[:, t, 0:512]),
                   reads=[f"x:{t}"], writes=["st6"])
            sq.add("vector", lambda e, t=t: e.bn_stats(stats6[:, 1, :], sx[:, t, 512:1024]),
                   reads=[f"x:{t}"], writes=["st6"])
            sq.add("vector", lambda e: e.bn_aggr(stats2[:], stats6[:]),
                   reads=["st6"], writes=["st2"])
            sq.add("vector", lambda e: e.tensor_scalar_add(stmp1[:], stats2[:, 1:2], 1e-5),
                   reads=["st2"], writes=["t1"])
            sq.add("scalar", lambda e: e.activation(stmp1b[:], stmp1[:], AF.Sqrt, bias=F0, scale=F1),
                   reads=["t1"], writes=["t1b"])
            sq.add("vector", lambda e, t=t: e.reciprocal(srcol[:, t:t + 1], stmp1b[:]),
                   reads=["t1b"], writes=["r"])
            sq.add("vector", lambda e, t=t: e.tensor_scalar(
                snmr[:, t:t + 1], stats2[:, 0:1], srcol[:, t:t + 1], -1.0,
                ALU.mult, ALU.mult),
                reads=["st2", "r"], writes=["nmr"])

            sq.add("scalar", lambda e, t=t, sytile=sytile: e.activation(
                sytile, sx[:, t, :], AF.Identity,
                scale=srcol[:, t:t + 1], bias=snmr[:, t:t + 1]),
                reads=[f"x:{t}", "r", "nmr"], writes=[sytag])

        with ExitStack() as a1:
            arena_mode = "noarena" not in skip

            def sb1(name, shape, dtp):
                if arena_mode:
                    return a1.enter_context(nc.sbuf_tensor(name, shape, dtp, side="right"))
                return ctx.enter_context(nc.sbuf_tensor(name, shape, dtp))

            def ps1(name, shape):
                if arena_mode:
                    return a1.enter_context(nc.psum_tensor(name, shape, F32))
                return ctx.enter_context(nc.psum_tensor(name, shape, F32))

            swm = sb1("swm", [128, 2, 3072], F32R)
            sy = sb1("sy", [128, 2, D], BF16)
            strn = sb1("strn", [128, 2, 1024], BF16)
            srow = sb1("srow", [1, D], F32)
            srow2 = sb1("srow2", [1, D], F32)
            srowb = sb1("srowb", [1, D], BF16)
            srel = sb1("srel", [32, 16], F32R)
            soh = sb1("soh", [32, 128], F32R)
            st5 = sb1("st5", [16, 128], F32)
            stau = sb1("stau", [16, 384], F32)
            staub = sb1("staub", [16, 384], BF16)
            srep = sb1("srep", [128, 384], BF16)
            s16 = sb1("s16", [16, 128], F32)
            sidf = sb1("sidf", [16, 16], F32)
            srowst = sb1("srowst", [1, 2, 512], F32)
            p_mod = ps1("p_mod", [1, 3072])
            p_gen = ps1("p_gen", [128, 512])
            p_c16 = ps1("p_c16", [128, 16])

            for t in range(NT):
                sq.add("sync", lambda e, t=t: e.dma_start(
                    out=sx[:, t, :], in_=x_in[128 * t:128 * (t + 1), :]),
                    writes=[f"x:{t}"], dma=True)
            sq.add("sync", lambda e: e.dma_start(out=sccol[:], in_=ccol[:]),
                   writes=["ccol"], dma=True)
            sq.enabled = "misc" not in skip
            sq.add("sync", lambda e: e.dma_start(out=sbqc[:], in_=bqc_d[:]),
                   writes=["bqc"], dma=True)
            sq.add("sync", lambda e: e.dma_start(out=sbkc[:], in_=bkc_d[:]),
                   writes=["bkc"], dma=True)
            sq.add("sync", lambda e: e.dma_start(out=srow2[:], in_=bvr[:]),
                   writes=["srow2"], dma=True)
            sq.add("vector", lambda e: e.tensor_copy(sbvb[:], srow2[:]),
                   reads=["srow2"], writes=["bvb"])
            sq.add("sync", lambda e: e.dma_start(out=srow[:], in_=bor[:]),
                   writes=["srow"], dma=True)
            sq.add("vector", lambda e: e.tensor_copy(sbob[:], srow[:]),
                   reads=["srow"], writes=["bob"])
            sq.add("gpsimd", lambda e: e.dma_start(out=srel[:], in_=rel[:]),
                   writes=["rel"], dma=True)
            sq.add("gpsimd", lambda e: e.dma_start(out=soh[:], in_=ohd[:]),
                   writes=["oh"], dma=True)
            sq.add("sync", lambda e: e.dma_start(out=sidf[:], in_=idf[:]),
                   writes=["idf"], dma=True)
            sq.add("vector", lambda e: e.memset(ones1[:], 1.0), writes=["ones1"])
            sq.add("sync", lambda e: e.dma_start(out=srowst[0:1, 0, 0:1], in_=wsc[:]),
                   writes=["wscrow"], dma=True)
            sq.add("vector", lambda e: e.tensor_copy(swscb[:], srowst[0:1, 0, 0:1]),
                   reads=["wscrow"], writes=["wscb"])
            sq.add("tensor", lambda e: e.matmul(
                p_gen[:, 0:1], ones1[:], swscb[:], start=True, stop=True),
                reads=["wscb", "ones1"], writes=["p_gen"])
            sq.add("vector", lambda e: e.tensor_copy(s_swb[:], p_gen[:, 0:1]),
                   reads=["p_gen"], writes=["swb"])

            sq.enabled = True
            sq.add("scalar", lambda e: e.activation(ssil[:], sccol[:], AF.Silu, bias=F0, scale=F1),
                   reads=["ccol"], writes=["sil"])
            sq.add("vector", lambda e: e.tensor_copy(scsil[:], ssil[:]),
                   reads=["sil"], writes=["csil"])

            for j in range(2):
                for c in range(NCH):
                    sq.add("gpsimd", lambda e, j=j, c=c: e.dma_start(
                        out=swm[:, c % 2, :],
                        in_=wm[128 * c:128 * (c + 1), 3072 * j:3072 * (j + 1)]),
                        writes=[f"swm:{c % 2}"], dma=True)
                    for sl in range(6):
                        sq.add("tensor", lambda e, c=c, sl=sl: e.matmul(
                            p_mod[:, 512 * sl:512 * (sl + 1)],
                            scsil[:, c:c + 1],
                            swm[:, c % 2, 512 * sl:512 * (sl + 1)],
                            start=(c == 0), stop=(c == NCH - 1)),
                            reads=["csil", f"swm:{c % 2}"],
                            writes=[f"p_mod:{sl}"])
                for sl in range(6):
                    sq.add("vector", lambda e, sl=sl: e.tensor_copy(
                        srowst[0:1, sl % 2, :], p_mod[:, 512 * sl:512 * (sl + 1)]),
                        reads=[f"p_mod:{sl}"], writes=[f"srowst:{sl % 2}"])
                    sq.add("sync", lambda e, j=j, sl=sl: e.dma_start(
                        out=drows[:, 3072 * j + 512 * sl:3072 * j + 512 * (sl + 1)],
                        in_=srowst[0:1, sl % 2, :]),
                        reads=[f"srowst:{sl % 2}"], writes=[f"dr:{j}:{sl}"], dma=True)
            DR = [f"dr:{j}:{sl}" for j in range(2) for sl in range(6)]

            sq.enabled = "cols" not in skip
            sq.add("sync", lambda e: e.dma_start(
                out=s16[:], in_=drows[:, 0:2048].rearrange("o (r j) -> (o r) j", j=128)),
                reads=DR[0:4], writes=["s16"], dma=True)
            sq.add("tensor", lambda e: e.transpose(p_c16[:], s16[:], sidf[:]),
                   reads=["s16", "idf"], writes=["p_c16"])
            sq.add("vector", lambda e: e.tensor_copy(a1c[:], p_c16[:]),
                   reads=["p_c16"], writes=["a1c"])
            if debug:
                sq.add("gpsimd", lambda e: e.dma_start(out=dbg_s16[:], in_=s16[:]),
                       reads=["s16"], writes=["dbg"], dma=True)
                sq.add("gpsimd", lambda e: e.dma_start(out=dbg_c16[:], in_=a1c[:]),
                       reads=["a1c"], writes=["dbg"], dma=True)
            sq.add("vector", lambda e: e.tensor_scalar_add(
                a1c[:, 8:16], a1c[:, 8:16], 1.0),
                reads=["a1c"], writes=["a1c"])

            sq.enabled = True

            def load_row(offs, deps):
                sq.add("sync", lambda e, offs=offs: e.dma_start(
                    out=srow[:], in_=drows[:, offs:offs + D]),
                    reads=list(deps), writes=["srow"], dma=True)

            def bcast_tile(dst, tag, plus1=False):
                for half in range(2):
                    sq.add("tensor", lambda e, half=half: e.matmul(
                        p_gen[:], ones1[:], srowb[:, 512 * half:512 * (half + 1)],
                        start=True, stop=True),
                        reads=["srowb", "ones1"], writes=["p_gen"])
                    bias = F1 if plus1 else F0
                    sq.add("scalar", lambda e, half=half, dst=dst, bias=bias: e.activation(
                        dst[:, 512 * half:512 * (half + 1)], p_gen[:], AF.Identity,
                        bias=bias, scale=F1),
                        reads=["p_gen"], writes=[tag])

            def clamp_gate(offs, deps):
                load_row(offs, deps)
                sq.add("scalar", lambda e: e.activation(srow2[:], srow[:], AF.Abs, bias=F0, scale=F1),
                       reads=["srow"], writes=["srow2"])
                sq.add("vector", lambda e: e.tensor_scalar_max(srow2[:], srow2[:], 0.25),
                       reads=["srow2"], writes=["srow2"])
                sq.add("vector", lambda e: e.tensor_scalar(
                    srow[:], srow[:], 0.0, None, ALU.is_ge),
                    reads=["srow"], writes=["srow"])
                sq.add("vector", lambda e: e.tensor_scalar(
                    srow[:], srow[:], 2.0, -1.0,
                    ALU.mult, ALU.add),
                    reads=["srow"], writes=["srow"])
                sq.add("vector", lambda e: e.tensor_mul(srow2[:], srow2[:], srow[:]),
                       reads=["srow"], writes=["srow2"])

            sq.enabled = "bcast" not in skip
            clamp_gate(2048, DR[4:6])
            sq.add("vector", lambda e: e.tensor_copy(srowb[:], srow2[:]),
                   reads=["srow2"], writes=["srowb"])
            bcast_tile(gat, "gat")
            clamp_gate(5120, DR[10:12])
            sq.add("vector", lambda e: e.tensor_copy(srowb[:], srow2[:]),
                   reads=["srow2"], writes=["srowb"])
            bcast_tile(gbw, "gbw")
            sq.add("sync", lambda e: e.dma_start(out=srow[:], in_=fbr[:]),
                   writes=["srow"], dma=True)
            sq.add("vector", lambda e: e.tensor_mul(srow2[:], srow2[:], srow[:]),
                   reads=["srow"], writes=["srow2"])
            sq.add("vector", lambda e: e.tensor_copy(srowb[:], srow2[:]),
                   reads=["srow2"], writes=["srowb"])
            bcast_tile(gbiast, "gbiast")
            load_row(4096, DR[8:10])
            sq.add("vector", lambda e: e.tensor_copy(srowb[:], srow[:]),
                   reads=["srow"], writes=["srowb"])
            bcast_tile(a2t, "a2t", plus1=True)
            load_row(3072, DR[6:8])
            sq.add("vector", lambda e: e.tensor_copy(srowb[:], srow[:]),
                   reads=["srow"], writes=["srowb"])
            bcast_tile(b2t, "b2t")

            sq.enabled = "t5" not in skip
            sq.add("tensor", lambda e: e.matmul(p_gen[0:16, 0:128], srel[:], soh[:],
                                                start=True, stop=True),
                   reads=["rel", "oh"], writes=["p_gen"])
            sq.add("vector", lambda e: e.tensor_copy(st5[:], p_gen[0:16, 0:128]),
                   reads=["p_gen"], writes=["t5"])
            sq.add("vector", lambda e: e.tensor_scalar(
                st5[:], st5[:], st5[:, 127:128], None, ALU.subtract),
                reads=["t5"], writes=["t5"])
            sq.add("vector", lambda e: e.memset(stau[:, 0:127], -1e30), writes=["tau"])
            sq.add("vector", lambda e: e.memset(stau[:, 240:384], 0.0), writes=["tau"])
            sq.add("vector", lambda e: e.tensor_copy(stau[:, 127:240], st5[:, 0:113]),
                   reads=["t5"], writes=["tau"])
            sq.add("scalar", lambda e: e.activation(stau[:], stau[:], AF.Exp, bias=F0, scale=F1),
                   reads=["tau"], writes=["tau"])
            sq.add("vector", lambda e: e.tensor_copy(staub[:], stau[:]),
                   reads=["tau"], writes=["taub"])
            sq.add("sync", lambda e: e.dma_start(out=dtaus[:], in_=staub[:]),
                   reads=["taub"], writes=["dtaus"], dma=True)
            for h in range(NH):
                sq.add("sync", lambda e, h=h: e.dma_start(
                    out=srowb[:, 0:384], in_=dtaus[h:h + 1, :]),
                    reads=["dtaus"], writes=["srowb"], dma=True)
                sq.add("tensor", lambda e: e.matmul(
                    p_gen[:, 0:384], ones1[:], srowb[:, 0:384], start=True, stop=True),
                    reads=["srowb", "ones1"], writes=["p_gen"])
                sq.add("vector", lambda e: e.tensor_copy(srep[:], p_gen[:, 0:384]),
                       reads=["p_gen"], writes=["srep"])
                sq.add("sync", lambda e, h=h: e.dma_start(
                    out=dtau[:, 384 * h:384 * (h + 1)], in_=srep[:]),
                    reads=["srep"], writes=[f"dtau:{h}"], dma=True)
            sq.enabled = True
            dbg_stats[0] = True
            for t in range(NT):
                b = t % 2
                ln_tile(t, sy[:, b, :], f"sy:{b}")
                sq.add("sync", lambda e, t=t, b=b: e.dma_start(
                    out=dy[128 * t:128 * (t + 1), :], in_=sy[:, b, :]),
                    reads=[f"sy:{b}"], writes=[f"dy:{t}"], dma=True)
            dbg_stats[0] = False
            sq.enabled = "trans" not in skip
            for half in range(2):
                tdeps = [f"dy:{t}" for t in range(8 * half, 8 * half + 8)]
                for c in range(NCH):
                    b = c % 2
                    sq.add("sync", lambda e, half=half, c=c, b=b: e.dma_start(
                        out=strn[:, b, :],
                        in_=dy[1024 * half:1024 * (half + 1), 128 * c:128 * (c + 1)],
                        transpose=True),
                        reads=tdeps, writes=[f"strn:{b}"], dma=True)
                    sq.add("scalar", lambda e, half=half, c=c, b=b: e.activation(
                        shT[:, c, 1024 * half:1024 * (half + 1)], strn[:, b, :],
                        AF.Identity, scale=a1c[:, 8 + c:9 + c], bias=a1c[:, c:c + 1]),
                        reads=[f"strn:{b}", "a1c"], writes=[f"hT:{c}:{half}"])

        sq.enabled = True
        if debug:
            sq.add("sync", lambda e: e.dma_start(
                out=dbg_x0[:], in_=sx[:].rearrange("p t d2 -> p (t d2)")),
                reads=XT, writes=["dbg"], dma=True)
            sq.add("sync", lambda e: e.dma_start(out=dbg_y[:], in_=dy[:]),
                   reads=[f"dy:{t}" for t in range(NT)], writes=["dbg"], dma=True)
            sq.add("sync", lambda e: e.dma_start(out=dbg_r[:, 0:NT], in_=srcol[:]),
                   reads=["r"], writes=["dbg"], dma=True)
            sq.add("sync", lambda e: e.dma_start(out=dbg_r[:, NT:2 * NT], in_=snmr[:]),
                   reads=["nmr"], writes=["dbg"], dma=True)
            sq.add("sync", lambda e: e.dma_start(out=dbg_st[:], in_=scoll[:]),
                   reads=["coll"], writes=["dbg"], dma=True)
            sq.add("sync", lambda e: e.dma_start(
                out=dbg_mod[:], in_=drows[:, 0:6 * D]),
                reads=DR, writes=["dbg"], dma=True)
            sq.add("sync", lambda e: e.dma_start(out=dbg_a1c[:], in_=a1c[:]),
                   reads=["a1c"], writes=["dbg"], dma=True)
            sq.add("sync", lambda e: e.dma_start(
                out=dbg_hT[:], in_=shT[:].rearrange("p c s -> p (c s)")),
                reads=[f"hT:{c}:{h}" for c in range(NCH) for h in range(2)],
                writes=["dbg"], dma=True)

        sq.barrier()

        if upto < 2:
            block = ctx.enter_context(nc.Block())
            sq.build(block, sems, serial=serial)
            return nc

        with ExitStack() as a2:
            sq.enabled = (upto >= 2)

            def sb2(name, shape, dtp):
                return a2.enter_context(nc.sbuf_tensor(name, shape, dtp, side="right"))

            def ps2(name, shape):
                return a2.enter_context(nc.psum_tensor(name, shape, F32))

            soT = sb2("soT", [128, NCH, SEQ], BF16)
            sqt = sb2("sqt", [128, SEQ], BF16)
            skt = sb2("skt", [128, SEQ], BF16)
            sv = sb2("sv", [128, NT, 130], BF16)
            swq = sb2("swq", [128, 2, NCH, 128], BF16)
            swk = sb2("swk", [128, 2, NCH, 128], BF16)
            swv = sb2("swv", [128, 2, NCH, 128], BF16)
            swo = sb2("swo", [128, NCH, D], BF16)
            sPt = sb2("sPt", [128, 4, 512], BF16)
            sE = sb2("sE", [128, NH, 256], BF16)
            srec = sb2("srec", [1, 512], F32)
            srecb = sb2("srecb", [1, 512], BF16)
            srbt = sb2("srbt", [64, 512], BF16)
            p_oacc = ps2("p_oacc", [65, SEQ])
            p_s = [ps2(f"p_s{i}", [128, 512]) for i in range(4)]
            sv4 = sv[:].rearrange("p t (g c) -> p t g c", c=65)

            for c in range(NCH):
                sq.add("gpsimd", lambda e, c=c: e.dma_start(
                    out=swo[:, c, :], in_=wo[128 * c:128 * (c + 1), :]),
                    writes=["swo"], dma=True)
            for h in range(NH):
                esrc = AP(dtau, 384 * h + 127, [[NH * 384 - 1, 128], [1, 256]])
                sq.add("sync", lambda e, h=h, esrc=esrc: e.dma_start(
                    out=sE[:, h, :], in_=esrc),
                    reads=[f"dtau:{h}"], writes=[f"E:{h}"], dma=True)

            for d in range(NP):
                wb = d % 2
                for wname, wt, stile in (("wq", wq, swq), ("wk", wk, swk),
                                         ("wv", wv, swv)):
                    for c in range(NCH):
                        sq.add("gpsimd", lambda e, wt=wt, stile=stile, d=d,
                               wb=wb, c=c: e.dma_start(
                                   out=stile[:, wb, c, :],
                                   in_=wt[128 * c:128 * (c + 1),
                                          128 * d:128 * (d + 1)]),
                               writes=[f"{wname}:{wb}"], dma=True)

                for which, wtag, stile, dst, dsttag, bc, btag in (
                        (0, "wq", swq, sqt, "sqt", sbqc, "bqc"),
                        (1, "wk", swk, skt, "skt", sbkc, "bkc")):
                    for qc in range(NQC):
                        pb = p_s[2 * which + (qc % 2)]
                        pt = f"ps:{2 * which + (qc % 2)}"
                        for c in range(NCH):
                            sq.add("tensor", lambda e, stile=stile, wb=wb, c=c,
                                   qc=qc, pb=pb: e.matmul(
                                       pb[:], stile[:, wb, c, :],
                                       shT[:, c, 512 * qc:512 * (qc + 1)],
                                       start=(c == 0), stop=(c == NCH - 1)),
                                   reads=[f"{wtag}:{wb}", f"hT:{c}:{qc // 2}"],
                                   writes=[pt])
                        sq.add("scalar", lambda e, dst=dst, qc=qc, pb=pb, bc=bc,
                               d=d: e.activation(
                                   dst[:, 512 * qc:512 * (qc + 1)], pb[:],
                                   AF.Identity, bias=bc[:, d:d + 1], scale=F1),
                               reads=[pt, btag], writes=[dsttag])

                sq.add("vector", lambda e: e.memset(sv4[:, :, :, 64:65], 1.0),
                       writes=["svones"])
                for t in range(NT):
                    pb = p_s[t % 2]
                    pt = f"ps:{t % 2}"
                    for c in range(NCH):
                        sq.add("tensor", lambda e, wb=wb, c=c, t=t, pb=pb: e.matmul(
                            pb[:, 0:128], shT[:, c, 128 * t:128 * (t + 1)],
                            swv[:, wb, c, :], start=(c == 0), stop=False),
                            reads=[f"wv:{wb}", f"hT:{c}:{t // 8}"], writes=[pt])
                    sq.add("tensor", lambda e, d=d, pb=pb: e.matmul(
                        pb[:, 0:128], ones1[:], sbvb[:, 128 * d:128 * (d + 1)],
                        start=False, stop=True),
                        reads=["bvb", "ones1"], writes=[pt])
                    sq.add("scalar", lambda e, t=t, pb=pb: e.activation(
                        sv4[:, t, :, 0:64],
                        pb[:, 0:128].rearrange("p (g c) -> p g c", c=64),
                        AF.Identity, bias=F0, scale=F1),
                        reads=[pt], writes=[f"sv:{t}"])

                if debug and d == 0:
                    sq.add("sync", lambda e: e.dma_start(out=dbg_qt[:], in_=sqt[:]),
                           reads=["sqt"], writes=["dbg"], dma=True)
                    sq.add("sync", lambda e: e.dma_start(out=dbg_kt[:], in_=skt[:]),
                           reads=["skt"], writes=["dbg"], dma=True)
                    sq.add("sync", lambda e: e.dma_start(
                        out=dbg_v[:], in_=sv[:].rearrange("p t g -> p (t g)")),
                        reads=[f"sv:{t}" for t in range(NT)] + ["svones"],
                        writes=["dbg"], dma=True)
                    sq.add("sync", lambda e: e.dma_start(
                        out=dbg_E[:], in_=sE[:, 0:2, :].rearrange("p h w -> p (h w)")),
                        reads=["E:0", "E:1"], writes=["dbg"], dma=True)

                for hb in range(2):
                    o = 64 * hb
                    h = 2 * d + hb
                    for qc in range(NQC):
                        nkt = 4 * qc + 4

                        def emit_qk(kt, qc=qc, o=o):
                            qoff = max(0, 128 * kt - 512 * qc)
                            pb = p_s[kt % 4]
                            sq.add("tensor", lambda e, kt=kt, qoff=qoff, pb=pb,
                                   o=o, qc=qc: e.matmul(
                                       pb[:, qoff:512],
                                       skt[o:o + 64, 128 * kt:128 * (kt + 1)],
                                       sqt[o:o + 64, 512 * qc + qoff:512 * (qc + 1)],
                                       start=True, stop=True, tile_position=(o, 0)),
                                   reads=["sqt", "skt"], writes=[f"ps:{kt % 4}"])

                        def emit_sm(kt, qc=qc, h=h):
                            qoff = max(0, 128 * kt - 512 * qc)
                            pb = p_s[kt % 4]
                            sq.add("scalar", lambda e, kt=kt, qoff=qoff, pb=pb: e.activation(
                                sPt[:, kt % 4, qoff:512], pb[:, qoff:512],
                                AF.Exp, bias=EXPB, scale=F1),
                                reads=[f"ps:{kt % 4}"], writes=[f"pt:{kt % 4}"])
                            j = kt - 4 * qc
                            if j >= 0:
                                w = min(240, 512 - 128 * j)
                                sq.add("vector", lambda e, kt=kt, j=j, w=w, h=h: e.tensor_mul(
                                    sPt[:, kt % 4, 128 * j:128 * j + w],
                                    sPt[:, kt % 4, 128 * j:128 * j + w],
                                    sE[:, h, 0:w]),
                                    reads=[f"E:{h}"], writes=[f"pt:{kt % 4}"])
                            elif j == -1:
                                sq.add("vector", lambda e, kt=kt, h=h: e.tensor_mul(
                                    sPt[:, kt % 4, 0:112], sPt[:, kt % 4, 0:112],
                                    sE[:, h, 128:240]),
                                    reads=[f"E:{h}"], writes=[f"pt:{kt % 4}"])

                        def emit_pv(kt, qc=qc, hb=hb, nkt=nkt):
                            qoff = max(0, 128 * kt - 512 * qc)
                            sq.add("tensor", lambda e, kt=kt, qoff=qoff, hb=hb,
                                   qc=qc, nkt=nkt: e.matmul(
                                       p_oacc[:, 512 * qc + qoff:512 * (qc + 1)],
                                       sv[:, kt, 65 * hb:65 * hb + 65],
                                       sPt[:, kt % 4, qoff:512],
                                       start=(kt == 0), stop=(kt == nkt - 1)),
                                   reads=[f"pt:{kt % 4}", f"sv:{kt}", "svones"],
                                   writes=["oacc"])

                        L = 2
                        for kt in range(min(L, nkt)):
                            emit_qk(kt)
                        for kt in range(nkt):
                            if kt + L < nkt:
                                emit_qk(kt + L)
                            emit_sm(kt)
                            emit_pv(kt)

                    for qc in range(NQC):
                        pb = p_s[qc % 2]
                        pt = f"ps:{qc % 2}"
                        sq.add("vector", lambda e, qc=qc: e.reciprocal(
                            srec[:], p_oacc[64:65, 512 * qc:512 * (qc + 1)]),
                            reads=["oacc"], writes=["srec"])
                        sq.add("vector", lambda e: e.tensor_copy(srecb[:], srec[:]),
                               reads=["srec"], writes=["srecb"])
                        sq.add("tensor", lambda e, pb=pb: e.matmul(
                            pb[0:64, :], ones1[:, 0:64],
                            srecb[:], start=True, stop=True),
                            reads=["srecb"], writes=[pt])
                        sq.add("vector", lambda e, pb=pb: e.tensor_copy(
                            srbt[:], pb[0:64, :]),
                            reads=[pt], writes=["srbt"])
                        sq.add("vector", lambda e, qc=qc, o=o, d=d: e.tensor_mul(
                            soT[o:o + 64, d, 512 * qc:512 * (qc + 1)],
                            p_oacc[0:64, 512 * qc:512 * (qc + 1)], srbt[:]),
                            reads=["oacc", "srbt"], writes=["soT"])

            if debug:
                sq.add("sync", lambda e: e.dma_start(
                    out=dbg_oT[:], in_=soT[:].rearrange("p c s -> p (c s)")),
                    reads=["soT"], writes=["dbg"], dma=True)

            for t in range(NT):
                for half in range(2):
                    pb = p_s[(2 * t + half) % 2]
                    pt = f"ps:{(2 * t + half) % 2}"
                    for c in range(NCH):
                        sq.add("tensor", lambda e, t=t, half=half, c=c, pb=pb: e.matmul(
                            pb[:], soT[:, c, 128 * t:128 * (t + 1)],
                            swo[:, c, 512 * half:512 * (half + 1)],
                            start=(c == 0), stop=False),
                            reads=["soT", "swo"], writes=[pt])
                    sq.add("tensor", lambda e, half=half, pb=pb: e.matmul(
                        pb[:], ones1[:], sbob[:, 512 * half:512 * (half + 1)],
                        start=False, stop=True),
                        reads=["bob", "ones1"], writes=[pt])
                    sq.add("vector", lambda e, half=half, pb=pb: e.scalar_tensor_tensor(
                        stmp[:], pb[:], 1.0, gat[:, 512 * half:512 * (half + 1)],
                        ALU.mult, ALU.mult),
                        reads=[pt, "gat"], writes=["stmp"])
                    sq.add("vector", lambda e, t=t, half=half: e.tensor_add(
                        sx[:, t, 512 * half:512 * (half + 1)],
                        sx[:, t, 512 * half:512 * (half + 1)], stmp[:]),
                        reads=["stmp"], writes=[f"x:{t}"])

        if debug:
            sq.add("sync", lambda e: e.dma_start(
                out=dbg_xa[:], in_=sx[:].rearrange("p t d2 -> p (t d2)")),
                reads=XT, writes=["dbg"], dma=True)

        sq.barrier()

        with ExitStack() as a3:
            sq.enabled = (upto >= 3)

            def sb3(name, shape, dtp):
                return a3.enter_context(nc.sbuf_tensor(name, shape, dtp, side="right"))

            def ps3(name, shape):
                return a3.enter_context(nc.psum_tensor(name, shape, F32))

            spk = sb3("spk", [128, NCH, 256], I32)
            swf = sb3("swf", [128, NCH, D], BF16)
            sy3 = sb3("sy3", [128, 2, D], BF16)
            sh2 = sb3("sh2", [128, 2, D], BF16)
            st32 = sb3("st32", [128, D], F32)
            sqi = sb3("sqi", [128, D], I32)
            sei = sb3("sei", [128, 256], I32)
            p_f = [ps3(f"p_f{i}", [128, 512]) for i in range(2)]

            for c in range(NCH):
                sq.add("sync", lambda e, c=c: e.dma_start(
                    out=spk[:, c, :], in_=pkt[128 * c:128 * (c + 1), :]),
                    writes=["spk"], dma=True)
            for c in range(NCH):
                for s in range(4):
                    sq.add("vector", lambda e, c=c, s=s: e.tensor_scalar(
                        sei[:], spk[:, c, :], 2 * s, 3,
                        ALU.logical_shift_right, ALU.bitwise_and),
                        reads=["spk"], writes=["sei"])
                    sq.add("vector", lambda e, c=c, s=s: e.tensor_scalar(
                        swf[:, c, 256 * s:256 * (s + 1)], sei[:], 1, None,
                        ALU.subtract),
                        reads=["sei"], writes=[f"swf:{c}"])
                sq.add("vector", lambda e, c=c: e.tensor_mul(
                    swf[:, c, :], swf[:, c, :], gbw[:]),
                    reads=["gbw"], writes=[f"swf:{c}"])

            for t in range(NT):
                b = t % 2
                ln_tile(t, sy3[:, b, :], f"sy:{b}")
                sq.add("vector", lambda e, b=b: e.tensor_mul(
                    sh2[:, b, :], sy3[:, b, :], a2t[:]),
                    reads=[f"sy:{b}", "a2t"], writes=[f"sh2:{b}"])
                sq.add("vector", lambda e, b=b: e.tensor_add(
                    sh2[:, b, :], sh2[:, b, :], b2t[:]),
                    reads=["b2t"], writes=[f"sh2:{b}"])
                sq.add("vector", lambda e, b=b: e.reduce_max(
                    stmp1[:], sh2[:, b, :], AXX, apply_absolute_value=True),
                    reads=[f"sh2:{b}"], writes=["t1"])
                sq.add("vector", lambda e: e.tensor_scalar_max(stmp1[:], stmp1[:], 1e-5),
                       reads=["t1"], writes=["t1"])
                sq.add("vector", lambda e, t=t: e.tensor_scalar(
                    skap[:, t:t + 1], stmp1[:], s_swb[:], 1.0 / 127.0,
                    ALU.mult, ALU.mult),
                    reads=["t1", "swb"], writes=["kap"])
                sq.add("vector", lambda e: e.reciprocal(stmp1b[:], stmp1[:]),
                       reads=["t1"], writes=["t1b"])
                sq.add("vector", lambda e, t=t: e.tensor_scalar_mul(
                    ssc[:, t:t + 1], stmp1b[:], 127.0),
                    reads=["t1b"], writes=["qsc"])
                sq.add("scalar", lambda e, t=t, b=b: e.activation(
                    st32[:], sh2[:, b, :], AF.Identity, scale=ssc[:, t:t + 1],
                    bias=F0),
                    reads=[f"sh2:{b}", "qsc"], writes=["st32"])
                sq.add("vector", lambda e: e.tensor_copy(sqi[:], st32[:]),
                       reads=["st32"], writes=["sqi"])
                sq.add("vector", lambda e, b=b: e.tensor_copy(sy3[:, b, :], sqi[:]),
                       reads=["sqi"], writes=[f"sy:{b}"])
                sq.add("sync", lambda e, t=t, b=b: e.dma_start(
                    out=dy[128 * t:128 * (t + 1), :], in_=sy3[:, b, :]),
                    reads=[f"sy:{b}"], writes=[f"dy:{t}"], dma=True)
            for half in range(2):
                tdeps = [f"dy:{t}" for t in range(8 * half, 8 * half + 8)]
                for c in range(NCH):
                    sq.add("sync", lambda e, half=half, c=c: e.dma_start(
                        out=shT[:, c, 1024 * half:1024 * (half + 1)],
                        in_=dy[1024 * half:1024 * (half + 1), 128 * c:128 * (c + 1)],
                        transpose=True),
                        reads=tdeps, writes=[f"hT:{c}:{half}"], dma=True)

            if debug:
                sq.add("sync", lambda e: e.dma_start(
                    out=dbg_wf[:], in_=swf[:].rearrange("p c d2 -> p (c d2)")),
                    reads=[f"swf:{c}" for c in range(NCH)], writes=["dbg"], dma=True)
                sq.add("sync", lambda e: e.dma_start(out=dbg_q[:], in_=dy[:]),
                       reads=[f"dy:{t}" for t in range(NT)], writes=["dbg"], dma=True)
                sq.add("sync", lambda e: e.dma_start(
                    out=dbg_kap[:, 0:NT], in_=skap[:]),
                    reads=["kap"], writes=["dbg"], dma=True)
                sq.add("sync", lambda e: e.dma_start(
                    out=dbg_kap[:, NT:2 * NT], in_=ssc[:]),
                    reads=["qsc"], writes=["dbg"], dma=True)

            for t in range(NT):
                for half in range(2):
                    pb = p_f[(2 * t + half) % 2]
                    pt = f"pf:{(2 * t + half) % 2}"
                    for c in range(NCH):
                        sq.add("tensor", lambda e, t=t, half=half, c=c, pb=pb: e.matmul(
                            pb[:], shT[:, c, 128 * t:128 * (t + 1)],
                            swf[:, c, 512 * half:512 * (half + 1)],
                            start=(c == 0), stop=(c == NCH - 1)),
                            reads=[f"hT:{c}:{t // 8}", f"swf:{c}"], writes=[pt]),
                    sq.add("vector", lambda e, t=t, half=half, pb=pb: e.scalar_tensor_tensor(
                        stmp[:], pb[:], skap[:, t:t + 1],
                        gbiast[:, 512 * half:512 * (half + 1)],
                        ALU.mult, ALU.add),
                        reads=[pt, "kap", "gbiast"], writes=["stmp"])
                    sq.add("vector", lambda e, t=t, half=half: e.tensor_add(
                        sx[:, t, 512 * half:512 * (half + 1)],
                        sx[:, t, 512 * half:512 * (half + 1)], stmp[:]),
                        reads=["stmp"], writes=[f"x:{t}"])

            for t in range(NT):
                sq.add("sync", lambda e, t=t: e.dma_start(
                    out=out[128 * t:128 * (t + 1), :], in_=sx[:, t, :]),
                    reads=[f"x:{t}"], writes=["outdma"], dma=True)
            sq.add("vector", lambda e: e.engine_nop(), reads=["outdma"],
                   writes=["done"])

        block = ctx.enter_context(nc.Block())
        sq.build(block, sems, serial=serial)
    return nc


_CACHED = {}


def _get_nc():
    if "nc" not in _CACHED:
        _CACHED["nc"] = build_nc()
    return _CACHED["nc"]


def kernel(L0_attn_x, L0_attn_Wq, L0_attn_bq, L0_attn_Wk, L0_attn_bk,
           L0_attn_Wv, L0_attn_bv, L0_attn_Wo, L0_attn_bo, L0_attn_rel_emb,
           L0_ffn_packed_weight, L0_ffn_weight_scale, L0_ffn_bias, L0_c,
           L0_wmod):
    x = np.ascontiguousarray(np.asarray(L0_attn_x, np.float32))
    wqT = np.ascontiguousarray(np.asarray(L0_attn_Wq, np.float32).T)
    wkT = np.ascontiguousarray(np.asarray(L0_attn_Wk, np.float32).T)
    wvT = np.ascontiguousarray(np.asarray(L0_attn_Wv, np.float32).T)
    woT = np.ascontiguousarray(np.asarray(L0_attn_Wo, np.float32).T)
    wmv = np.ascontiguousarray(np.asarray(L0_wmod, np.float32))
    pkT = np.ascontiguousarray(
        np.asarray(L0_ffn_packed_weight, np.int32).reshape(256, 1024).T)
    relv = np.ascontiguousarray(np.asarray(L0_attn_rel_emb, np.float32))
    ohv = _oh_const()
    c = np.asarray(L0_c, np.float32)
    bqcv = np.ascontiguousarray(np.asarray(L0_attn_bq, np.float32).reshape(8, 128).T)
    bkcv = np.ascontiguousarray(np.asarray(L0_attn_bk, np.float32).reshape(8, 128).T)
    bvr = np.ascontiguousarray(np.asarray(L0_attn_bv, np.float32).reshape(1, D))
    borv = np.ascontiguousarray(np.asarray(L0_attn_bo, np.float32).reshape(1, D))
    fbrv = np.ascontiguousarray(np.asarray(L0_ffn_bias, np.float32).reshape(1, D))
    wscv = np.asarray(L0_ffn_weight_scale, np.float32).reshape(1, 1)
    eye16 = np.eye(16, dtype=np.float32)

    nc = _get_nc()
    in_maps = []
    for b in range(8):
        in_maps.append({
            "x": x[b], "wq": wqT, "wk": wkT, "wv": wvT, "wo": woT,
            "wm": wmv, "pk": pkT,
            "ccol": np.ascontiguousarray(c[b].reshape(8, 128).T),
            "bqc": bqcv, "bkc": bkcv, "bvr": bvr, "bor": borv, "fbr": fbrv,
            "rel": relv, "oh": ohv, "wsc": wscv, "idf": eye16,
        })
    res = run_bass_kernel_spmd(nc, in_maps, core_ids=list(range(8)))
    return np.stack([r["out"] for r in res.results], axis=0)


# version 03
# speedup vs baseline: 1.2504x; 1.2504x over previous
import math
from contextlib import ExitStack

import numpy as np

import concourse.bass as bass
import concourse.mybir as mybir
from concourse.ap import AP
from concourse.bass_utils import run_bass_kernel_spmd

F32 = mybir.dt.float32
F32R = mybir.dt.float32r
BF16 = mybir.dt.bfloat16
I32 = mybir.dt.int32
AF = mybir.ActivationFunctionType
ALU = mybir.AluOpType
AXX = mybir.AxisListType.X

SEQ = 2048
D = 1024
NH = 16
NT = SEQ // 128
NCH = D // 128
NP = NH // 2
NQC = SEQ // 512
EXPB = np.float32(-12.0)
F1 = np.float32(1.0)
F0 = np.float32(0.0)


class Sched:
    ENGINES = ("tensor", "vector", "scalar", "gpsimd", "sync")
    NDMA = 8

    def __init__(self):
        self.steps = []

    enabled = True

    def add(self, eng, fn, reads=(), writes=(), dma=False):
        if not self.enabled:
            return
        self.steps.append([eng, fn, tuple(reads), tuple(writes), bool(dma)])

    def barrier(self):
        self.steps.append(["BARRIER", None, (), (), False])

    def build(self, block, sems, serial=False):
        steps = self.steps
        n = len(steps)
        cpos, dpos = {}, {}
        ccount = {e: 0 for e in self.ENGINES}
        dcount = {e: 0 for e in self.ENGINES}
        deps = [set() for _ in range(n)]
        last_w, readers = {}, {}
        dqueues = {}
        bar_after = [None] * n
        snap = None
        for i, (eng, fn, rd, wr, dma) in enumerate(steps):
            if eng == "BARRIER":
                snap = (dict(ccount), dict(dcount))
                continue
            bar_after[i] = snap
            if dma:
                dpos[i] = dcount[eng]
                dcount[eng] += 1
            else:
                cpos[i] = ccount[eng]
                ccount[eng] += 1
            d = deps[i]
            for r in rd:
                j = last_w.get(r)
                if j is not None:
                    d.add(j)
            for w in wr:
                j = last_w.get(w)
                if j is not None:
                    d.add(j)
                for j in readers.get(w, ()):
                    d.add(j)
            if dma:
                q = dqueues.setdefault(eng, [])
                if len(q) >= self.NDMA:
                    d.add(q[-self.NDMA])
                q.append(i)
            for w in wr:
                last_w[w] = i
                readers[w] = []
            for r in rd:
                readers.setdefault(r, []).append(i)

        sched = self

        def mk(engine_name):
            def emit(eng):
                watermark = {}

                def wait_val(key, sem, val):
                    if watermark.get(key, -1) >= val:
                        return
                    watermark[key] = val
                    eng.wait_ge(sem, val)

                def wait_step(j):
                    e2, _, _, _, dm2 = steps[j]
                    if dm2:
                        k = dpos[j]
                        key = f"d:{e2}:{k % sched.NDMA}"
                        wait_val(key, sems[key], 16 * (k // sched.NDMA + 1))
                    elif e2 != engine_name or e2 != "tensor":
                        key = f"c:{e2}"
                        wait_val(key, sems[key], cpos[j] + 1)

                emitted_bar = None
                run_cc = {e2: 0 for e2 in sched.ENGINES}
                run_dc = {e2: 0 for e2 in sched.ENGINES}
                for i, (e, fn, rd, wr, dma) in enumerate(steps):
                    if e == "BARRIER":
                        continue
                    if serial and e != engine_name:
                        if dma:
                            run_dc[e] += 1
                        else:
                            run_cc[e] += 1
                        continue
                    if serial:
                        for e2 in sched.ENGINES:
                            if run_cc[e2] > 0:
                                wait_val(f"c:{e2}", sems[f"c:{e2}"], run_cc[e2])
                            for kk in range(max(0, run_dc[e2] - sched.NDMA), run_dc[e2]):
                                key = f"d:{e2}:{kk % sched.NDMA}"
                                wait_val(key, sems[key],
                                         16 * (kk // sched.NDMA + 1))
                        if dma:
                            run_dc[e] += 1
                        else:
                            run_cc[e] += 1
                        ins = fn(eng)
                        if dma:
                            k = dpos[i]
                            ins.then_inc(sems[f"d:{engine_name}:{k % sched.NDMA}"], 16)
                        else:
                            ins.then_inc(sems[f"c:{engine_name}"], 1)
                        continue
                    if e != engine_name:
                        continue
                    if bar_after[i] is not None and bar_after[i] is not emitted_bar:
                        cc, dc = bar_after[i]
                        emitted_bar = bar_after[i]
                        for e2 in sched.ENGINES:
                            if e2 != engine_name and cc[e2] > 0:
                                wait_val(f"c:{e2}", sems[f"c:{e2}"], cc[e2])
                            for kk in range(max(0, dc[e2] - sched.NDMA), dc[e2]):
                                key = f"d:{e2}:{kk % sched.NDMA}"
                                wait_val(key, sems[key],
                                         16 * (kk // sched.NDMA + 1))
                    for j in sorted(deps[i]):
                        wait_step(j)
                    ins = fn(eng)
                    if dma:
                        k = dpos[i]
                        ins.then_inc(sems[f"d:{engine_name}:{k % sched.NDMA}"], 16)
                    else:
                        ins.then_inc(sems[f"c:{engine_name}"], 1)
            return emit

        block.tensor(mk("tensor"))
        block.vector(mk("vector"))
        block.scalar(mk("scalar"))
        block.gpsimd(mk("gpsimd"))
        block.sync(mk("sync"))


def _oh_const():
    d = np.arange(128)
    t = np.log(np.maximum(d, 1).astype(np.float32) / np.float32(16.0))
    t = (t / np.float32(math.log(8.0))).astype(np.float32)
    t = (t * np.float32(16.0)).astype(np.float32)
    vl = np.minimum(16 + t.astype(np.int32), 31)
    bucket = np.where(d < 16, d, vl)
    oh = np.zeros((32, 128), np.float32)
    scale = np.float32((D // NH) ** (-0.5))
    for dlt in range(128):
        oh[bucket[dlt], dlt] = scale
    return oh


def build_nc(debug=False, serial=False, upto=3, skip=()):
    nc = bass.Bass("TRN2", target_bir_lowering=False)
    dt = nc.dram_tensor
    if debug:
        dbg_mod = dt("dbg_mod", [1, 6 * D], F32, kind="ExternalOutput")
        dbg_a1c = dt("dbg_a1c", [128, 16], F32, kind="ExternalOutput")
        dbg_hT = dt("dbg_hT", [128, NCH * SEQ], BF16, kind="ExternalOutput")
        dbg_E = dt("dbg_E", [128, 2 * 256], BF16, kind="ExternalOutput")
        dbg_qt = dt("dbg_qt", [128, SEQ], BF16, kind="ExternalOutput")
        dbg_kt = dt("dbg_kt", [128, SEQ], BF16, kind="ExternalOutput")
        dbg_v = dt("dbg_v", [128, NT * 130], BF16, kind="ExternalOutput")
        dbg_oT = dt("dbg_oT", [128, NCH * SEQ], BF16, kind="ExternalOutput")
        dbg_xa = dt("dbg_xa", [128, NT * D], F32, kind="ExternalOutput")
        dbg_wf = dt("dbg_wf", [128, NCH * D], BF16, kind="ExternalOutput")
        dbg_q = dt("dbg_q", [SEQ, D], BF16, kind="ExternalOutput")
        dbg_kap = dt("dbg_kap", [128, 2 * NT], F32, kind="ExternalOutput")
        dbg_y = dt("dbg_y", [SEQ, D], BF16, kind="ExternalOutput")
        dbg_x0 = dt("dbg_x0", [128, NT * D], F32, kind="ExternalOutput")
        dbg_s16 = dt("dbg_s16", [16, 128], F32, kind="ExternalOutput")
        dbg_c16 = dt("dbg_c16", [128, 16], F32, kind="ExternalOutput")
        dbg_st = dt("dbg_st", [128, 2 * NT], F32, kind="ExternalOutput")
        dbg_r = dt("dbg_r", [128, 2 * NT], F32, kind="ExternalOutput")
    x_in = dt("x", [SEQ, D], F32, kind="ExternalInput")
    wq = dt("wq", [D, D], F32, kind="ExternalInput")
    wk = dt("wk", [D, D], F32, kind="ExternalInput")
    wv = dt("wv", [D, D], F32, kind="ExternalInput")
    wo = dt("wo", [D, D], F32, kind="ExternalInput")
    wm = dt("wm", [D, 6 * D], F32, kind="ExternalInput")
    pkt = dt("pk", [D, 256], I32, kind="ExternalInput")
    ccol = dt("ccol", [128, 8], F32, kind="ExternalInput")
    bqc_d = dt("bqc", [128, 8], F32, kind="ExternalInput")
    bkc_d = dt("bkc", [128, 8], F32, kind="ExternalInput")
    bvr = dt("bvr", [1, D], F32, kind="ExternalInput")
    bor = dt("bor", [1, D], F32, kind="ExternalInput")
    fbr = dt("fbr", [1, D], F32, kind="ExternalInput")
    rel = dt("rel", [32, 16], F32, kind="ExternalInput")
    ohd = dt("oh", [32, 128], F32, kind="ExternalInput")
    wsc = dt("wsc", [1, 1], F32, kind="ExternalInput")
    idf = dt("idf", [16, 16], F32, kind="ExternalInput")
    out = dt("out", [SEQ, D], F32, kind="ExternalOutput")
    dy = dt("dy", [SEQ, D], BF16, kind="Internal")
    drows = dt("drows", [1, 8 * D], F32, kind="Internal")
    dtaus = dt("dtaus", [16, 384], BF16, kind="Internal")
    dtau = dt("dtau", [128, NH * 384], BF16, kind="Internal")

    sq = Sched()
    with ExitStack() as ctx:
        def sb(name, shape, dtp, **kw):
            return ctx.enter_context(nc.sbuf_tensor(name, shape, dtp, **kw))

        sx = sb("sx", [128, NT, D], F32)
        shT = sb("shT", [128, NCH, SEQ], BF16)
        gat = sb("gat", [128, D], BF16)
        gbiast = sb("gbiast", [128, D], BF16)
        a2t = sb("a2t", [128, D], BF16)
        b2t = sb("b2t", [128, D], BF16)
        gbw = sb("gbw", [128, D], BF16)
        sccol = sb("sccol", [128, 8], F32)
        ssil = sb("ssil", [128, 8], F32)
        scsil = sb("scsil", [128, 8], F32R)
        sbqc = sb("sbqc", [128, 8], F32)
        sbkc = sb("sbkc", [128, 8], F32)
        a1c = sb("a1c", [128, 16], F32)
        s_swb = sb("s_swb", [128, 1], F32)
        sbvb = sb("sbvb", [1, D], BF16)
        sbob = sb("sbob", [1, D], BF16)
        swscb = sb("swscb", [1, 1], BF16)
        ones1 = sb("ones1", [1, 128], BF16)
        stats6 = sb("stats6", [128, 2, 6], F32)
        stats2 = sb("stats2", [128, 2], F32)
        srcol = sb("srcol", [128, NT], F32)
        snmr = sb("snmr", [128, NT], F32)
        skap = sb("skap", [128, NT], F32)
        ssc = sb("ssc", [128, NT], F32)
        stmp1 = sb("stmp1", [128, 1], F32)
        stmp1b = sb("stmp1b", [128, 1], F32)
        stmp = sb("stmp", [128, 512], F32)
        scoll = sb("scoll", [128, 2 * NT], F32)

        sems = {}
        for e in Sched.ENGINES:
            sems[f"c:{e}"] = ctx.enter_context(nc.semaphore(f"c_{e}"))
        for e in ("sync", "gpsimd"):
            for k in range(Sched.NDMA):
                sems[f"d:{e}:{k}"] = ctx.enter_context(nc.semaphore(f"d_{e}_{k}"))

        XT = [f"x:{t}" for t in range(NT)]

        dbg_stats = [False]

        def ln_tile(t, sytile, sytag):
            sq.add("vector", lambda e, t=t: e.bn_stats(stats6[:, 0, :], sx[:, t, 0:512]),
                   reads=[f"x:{t}"], writes=["st6"])
            sq.add("vector", lambda e, t=t: e.bn_stats(stats6[:, 1, :], sx[:, t, 512:1024]),
                   reads=[f"x:{t}"], writes=["st6"])
            sq.add("vector", lambda e: e.bn_aggr(stats2[:], stats6[:]),
                   reads=["st6"], writes=["st2"])
            sq.add("vector", lambda e: e.tensor_scalar_add(stmp1[:], stats2[:, 1:2], 1e-5),
                   reads=["st2"], writes=["t1"])
            sq.add("scalar", lambda e: e.activation(stmp1b[:], stmp1[:], AF.Sqrt, bias=F0, scale=F1),
                   reads=["t1"], writes=["t1b"])
            sq.add("vector", lambda e, t=t: e.reciprocal(srcol[:, t:t + 1], stmp1b[:]),
                   reads=["t1b"], writes=["r"])
            sq.add("vector", lambda e, t=t: e.tensor_scalar(
                snmr[:, t:t + 1], stats2[:, 0:1], srcol[:, t:t + 1], -1.0,
                ALU.mult, ALU.mult),
                reads=["st2", "r"], writes=["nmr"])

            sq.add("scalar", lambda e, t=t, sytile=sytile: e.activation(
                sytile, sx[:, t, :], AF.Identity,
                scale=srcol[:, t:t + 1], bias=snmr[:, t:t + 1]),
                reads=[f"x:{t}", "r", "nmr"], writes=[sytag])

        with ExitStack() as a1:
            arena_mode = "noarena" not in skip

            def sb1(name, shape, dtp):
                if arena_mode:
                    return a1.enter_context(nc.sbuf_tensor(name, shape, dtp, side="right"))
                return ctx.enter_context(nc.sbuf_tensor(name, shape, dtp))

            def ps1(name, shape):
                if arena_mode:
                    return a1.enter_context(nc.psum_tensor(name, shape, F32))
                return ctx.enter_context(nc.psum_tensor(name, shape, F32))

            swm = sb1("swm", [128, 2, 3072], F32R)
            sy = sb1("sy", [128, 2, D], BF16)
            strn = sb1("strn", [128, 2, 1024], BF16)
            srow = sb1("srow", [1, D], F32)
            srow2 = sb1("srow2", [1, D], F32)
            srowb = sb1("srowb", [1, D], BF16)
            srel = sb1("srel", [32, 16], F32R)
            soh = sb1("soh", [32, 128], F32R)
            st5 = sb1("st5", [16, 128], F32)
            stau = sb1("stau", [16, 384], F32)
            staub = sb1("staub", [16, 384], BF16)
            srep = sb1("srep", [128, 384], BF16)
            s16 = sb1("s16", [16, 128], F32)
            sidf = sb1("sidf", [16, 16], F32)
            srowst = sb1("srowst", [1, 2, 512], F32)
            p_mod = ps1("p_mod", [1, 3072])
            p_gen = ps1("p_gen", [128, 512])
            p_c16 = ps1("p_c16", [128, 16])

            for t in range(NT):
                sq.add("sync", lambda e, t=t: e.dma_start(
                    out=sx[:, t, :], in_=x_in[128 * t:128 * (t + 1), :]),
                    writes=[f"x:{t}"], dma=True)
            sq.add("sync", lambda e: e.dma_start(out=sccol[:], in_=ccol[:]),
                   writes=["ccol"], dma=True)
            sq.enabled = "misc" not in skip
            sq.add("sync", lambda e: e.dma_start(out=sbqc[:], in_=bqc_d[:]),
                   writes=["bqc"], dma=True)
            sq.add("sync", lambda e: e.dma_start(out=sbkc[:], in_=bkc_d[:]),
                   writes=["bkc"], dma=True)
            sq.add("sync", lambda e: e.dma_start(out=srow2[:], in_=bvr[:]),
                   writes=["srow2"], dma=True)
            sq.add("vector", lambda e: e.tensor_copy(sbvb[:], srow2[:]),
                   reads=["srow2"], writes=["bvb"])
            sq.add("sync", lambda e: e.dma_start(out=srow[:], in_=bor[:]),
                   writes=["srow"], dma=True)
            sq.add("vector", lambda e: e.tensor_copy(sbob[:], srow[:]),
                   reads=["srow"], writes=["bob"])
            sq.add("gpsimd", lambda e: e.dma_start(out=srel[:], in_=rel[:]),
                   writes=["rel"], dma=True)
            sq.add("gpsimd", lambda e: e.dma_start(out=soh[:], in_=ohd[:]),
                   writes=["oh"], dma=True)
            sq.add("sync", lambda e: e.dma_start(out=sidf[:], in_=idf[:]),
                   writes=["idf"], dma=True)
            sq.add("vector", lambda e: e.memset(ones1[:], 1.0), writes=["ones1"])
            sq.add("sync", lambda e: e.dma_start(out=srowst[0:1, 0, 0:1], in_=wsc[:]),
                   writes=["wscrow"], dma=True)
            sq.add("vector", lambda e: e.tensor_copy(swscb[:], srowst[0:1, 0, 0:1]),
                   reads=["wscrow"], writes=["wscb"])
            sq.add("tensor", lambda e: e.matmul(
                p_gen[:, 0:1], ones1[:], swscb[:], start=True, stop=True),
                reads=["wscb", "ones1"], writes=["p_gen"])
            sq.add("vector", lambda e: e.tensor_copy(s_swb[:], p_gen[:, 0:1]),
                   reads=["p_gen"], writes=["swb"])

            sq.enabled = True
            sq.add("scalar", lambda e: e.activation(ssil[:], sccol[:], AF.Silu, bias=F0, scale=F1),
                   reads=["ccol"], writes=["sil"])
            sq.add("vector", lambda e: e.tensor_copy(scsil[:], ssil[:]),
                   reads=["sil"], writes=["csil"])

            for j in range(2):
                for c in range(NCH):
                    sq.add("gpsimd", lambda e, j=j, c=c: e.dma_start(
                        out=swm[:, c % 2, :],
                        in_=wm[128 * c:128 * (c + 1), 3072 * j:3072 * (j + 1)]),
                        writes=[f"swm:{c % 2}"], dma=True)
                    for sl in range(6):
                        sq.add("tensor", lambda e, c=c, sl=sl: e.matmul(
                            p_mod[:, 512 * sl:512 * (sl + 1)],
                            scsil[:, c:c + 1],
                            swm[:, c % 2, 512 * sl:512 * (sl + 1)],
                            start=(c == 0), stop=(c == NCH - 1)),
                            reads=["csil", f"swm:{c % 2}"],
                            writes=[f"p_mod:{sl}"])
                for sl in range(6):
                    sq.add("vector", lambda e, sl=sl: e.tensor_copy(
                        srowst[0:1, sl % 2, :], p_mod[:, 512 * sl:512 * (sl + 1)]),
                        reads=[f"p_mod:{sl}"], writes=[f"srowst:{sl % 2}"])
                    sq.add("sync", lambda e, j=j, sl=sl: e.dma_start(
                        out=drows[:, 3072 * j + 512 * sl:3072 * j + 512 * (sl + 1)],
                        in_=srowst[0:1, sl % 2, :]),
                        reads=[f"srowst:{sl % 2}"], writes=[f"dr:{j}:{sl}"], dma=True)
            DR = [f"dr:{j}:{sl}" for j in range(2) for sl in range(6)]

            sq.enabled = "cols" not in skip
            sq.add("sync", lambda e: e.dma_start(
                out=s16[:], in_=drows[:, 0:2048].rearrange("o (r j) -> (o r) j", j=128)),
                reads=DR[0:4], writes=["s16"], dma=True)
            sq.add("tensor", lambda e: e.transpose(p_c16[:], s16[:], sidf[:]),
                   reads=["s16", "idf"], writes=["p_c16"])
            sq.add("vector", lambda e: e.tensor_copy(a1c[:], p_c16[:]),
                   reads=["p_c16"], writes=["a1c"])
            if debug:
                sq.add("gpsimd", lambda e: e.dma_start(out=dbg_s16[:], in_=s16[:]),
                       reads=["s16"], writes=["dbg"], dma=True)
                sq.add("gpsimd", lambda e: e.dma_start(out=dbg_c16[:], in_=a1c[:]),
                       reads=["a1c"], writes=["dbg"], dma=True)
            sq.add("vector", lambda e: e.tensor_scalar_add(
                a1c[:, 8:16], a1c[:, 8:16], 1.0),
                reads=["a1c"], writes=["a1c"])

            sq.enabled = True

            def load_row(offs, deps):
                sq.add("sync", lambda e, offs=offs: e.dma_start(
                    out=srow[:], in_=drows[:, offs:offs + D]),
                    reads=list(deps), writes=["srow"], dma=True)

            def bcast_tile(dst, tag, plus1=False):
                for half in range(2):
                    sq.add("tensor", lambda e, half=half: e.matmul(
                        p_gen[:], ones1[:], srowb[:, 512 * half:512 * (half + 1)],
                        start=True, stop=True),
                        reads=["srowb", "ones1"], writes=["p_gen"])
                    bias = F1 if plus1 else F0
                    sq.add("scalar", lambda e, half=half, dst=dst, bias=bias: e.activation(
                        dst[:, 512 * half:512 * (half + 1)], p_gen[:], AF.Identity,
                        bias=bias, scale=F1),
                        reads=["p_gen"], writes=[tag])

            def clamp_gate(offs, deps):
                load_row(offs, deps)
                sq.add("scalar", lambda e: e.activation(srow2[:], srow[:], AF.Abs, bias=F0, scale=F1),
                       reads=["srow"], writes=["srow2"])
                sq.add("vector", lambda e: e.tensor_scalar_max(srow2[:], srow2[:], 0.25),
                       reads=["srow2"], writes=["srow2"])
                sq.add("vector", lambda e: e.tensor_scalar(
                    srow[:], srow[:], 0.0, None, ALU.is_ge),
                    reads=["srow"], writes=["srow"])
                sq.add("vector", lambda e: e.tensor_scalar(
                    srow[:], srow[:], 2.0, -1.0,
                    ALU.mult, ALU.add),
                    reads=["srow"], writes=["srow"])
                sq.add("vector", lambda e: e.tensor_mul(srow2[:], srow2[:], srow[:]),
                       reads=["srow"], writes=["srow2"])

            sq.enabled = "bcast" not in skip
            clamp_gate(2048, DR[4:6])
            sq.add("vector", lambda e: e.tensor_copy(srowb[:], srow2[:]),
                   reads=["srow2"], writes=["srowb"])
            bcast_tile(gat, "gat")
            clamp_gate(5120, DR[10:12])
            sq.add("vector", lambda e: e.tensor_copy(srowb[:], srow2[:]),
                   reads=["srow2"], writes=["srowb"])
            bcast_tile(gbw, "gbw")
            sq.add("sync", lambda e: e.dma_start(out=srow[:], in_=fbr[:]),
                   writes=["srow"], dma=True)
            sq.add("vector", lambda e: e.tensor_mul(srow2[:], srow2[:], srow[:]),
                   reads=["srow"], writes=["srow2"])
            sq.add("vector", lambda e: e.tensor_copy(srowb[:], srow2[:]),
                   reads=["srow2"], writes=["srowb"])
            bcast_tile(gbiast, "gbiast")
            load_row(4096, DR[8:10])
            sq.add("vector", lambda e: e.tensor_copy(srowb[:], srow[:]),
                   reads=["srow"], writes=["srowb"])
            bcast_tile(a2t, "a2t", plus1=True)
            load_row(3072, DR[6:8])
            sq.add("vector", lambda e: e.tensor_copy(srowb[:], srow[:]),
                   reads=["srow"], writes=["srowb"])
            bcast_tile(b2t, "b2t")

            sq.enabled = "t5" not in skip
            sq.add("tensor", lambda e: e.matmul(p_gen[0:16, 0:128], srel[:], soh[:],
                                                start=True, stop=True),
                   reads=["rel", "oh"], writes=["p_gen"])
            sq.add("vector", lambda e: e.tensor_copy(st5[:], p_gen[0:16, 0:128]),
                   reads=["p_gen"], writes=["t5"])
            sq.add("vector", lambda e: e.tensor_scalar(
                st5[:], st5[:], st5[:, 127:128], None, ALU.subtract),
                reads=["t5"], writes=["t5"])
            sq.add("vector", lambda e: e.memset(stau[:, 0:127], -1e30), writes=["tau"])
            sq.add("vector", lambda e: e.memset(stau[:, 240:384], 0.0), writes=["tau"])
            sq.add("vector", lambda e: e.tensor_copy(stau[:, 127:240], st5[:, 0:113]),
                   reads=["t5"], writes=["tau"])
            sq.add("scalar", lambda e: e.activation(stau[:], stau[:], AF.Exp, bias=F0, scale=F1),
                   reads=["tau"], writes=["tau"])
            sq.add("vector", lambda e: e.tensor_copy(staub[:], stau[:]),
                   reads=["tau"], writes=["taub"])
            sq.add("sync", lambda e: e.dma_start(out=dtaus[:], in_=staub[:]),
                   reads=["taub"], writes=["dtaus"], dma=True)
            for h in range(NH):
                sq.add("sync", lambda e, h=h: e.dma_start(
                    out=srowb[:, 0:384], in_=dtaus[h:h + 1, :]),
                    reads=["dtaus"], writes=["srowb"], dma=True)
                sq.add("tensor", lambda e: e.matmul(
                    p_gen[:, 0:384], ones1[:], srowb[:, 0:384], start=True, stop=True),
                    reads=["srowb", "ones1"], writes=["p_gen"])
                sq.add("vector", lambda e: e.tensor_copy(srep[:], p_gen[:, 0:384]),
                       reads=["p_gen"], writes=["srep"])
                sq.add("sync", lambda e, h=h: e.dma_start(
                    out=dtau[:, 384 * h:384 * (h + 1)], in_=srep[:]),
                    reads=["srep"], writes=[f"dtau:{h}"], dma=True)
            sq.enabled = True
            dbg_stats[0] = True
            for t in range(NT):
                b = t % 2
                ln_tile(t, sy[:, b, :], f"sy:{b}")
                sq.add("sync", lambda e, t=t, b=b: e.dma_start(
                    out=dy[128 * t:128 * (t + 1), :], in_=sy[:, b, :]),
                    reads=[f"sy:{b}"], writes=[f"dy:{t}"], dma=True)
            dbg_stats[0] = False
            sq.enabled = "trans" not in skip
            for half in range(2):
                tdeps = [f"dy:{t}" for t in range(8 * half, 8 * half + 8)]
                for c in range(NCH):
                    b = c % 2
                    sq.add("sync", lambda e, half=half, c=c, b=b: e.dma_start(
                        out=strn[:, b, :],
                        in_=dy[1024 * half:1024 * (half + 1), 128 * c:128 * (c + 1)],
                        transpose=True),
                        reads=tdeps, writes=[f"strn:{b}"], dma=True)
                    sq.add("scalar", lambda e, half=half, c=c, b=b: e.activation(
                        shT[:, c, 1024 * half:1024 * (half + 1)], strn[:, b, :],
                        AF.Identity, scale=a1c[:, 8 + c:9 + c], bias=a1c[:, c:c + 1]),
                        reads=[f"strn:{b}", "a1c"], writes=[f"hT:{c}:{half}"])

        sq.enabled = True
        if debug:
            sq.add("sync", lambda e: e.dma_start(
                out=dbg_x0[:], in_=sx[:].rearrange("p t d2 -> p (t d2)")),
                reads=XT, writes=["dbg"], dma=True)
            sq.add("sync", lambda e: e.dma_start(out=dbg_y[:], in_=dy[:]),
                   reads=[f"dy:{t}" for t in range(NT)], writes=["dbg"], dma=True)
            sq.add("sync", lambda e: e.dma_start(out=dbg_r[:, 0:NT], in_=srcol[:]),
                   reads=["r"], writes=["dbg"], dma=True)
            sq.add("sync", lambda e: e.dma_start(out=dbg_r[:, NT:2 * NT], in_=snmr[:]),
                   reads=["nmr"], writes=["dbg"], dma=True)
            sq.add("sync", lambda e: e.dma_start(out=dbg_st[:], in_=scoll[:]),
                   reads=["coll"], writes=["dbg"], dma=True)
            sq.add("sync", lambda e: e.dma_start(
                out=dbg_mod[:], in_=drows[:, 0:6 * D]),
                reads=DR, writes=["dbg"], dma=True)
            sq.add("sync", lambda e: e.dma_start(out=dbg_a1c[:], in_=a1c[:]),
                   reads=["a1c"], writes=["dbg"], dma=True)
            sq.add("sync", lambda e: e.dma_start(
                out=dbg_hT[:], in_=shT[:].rearrange("p c s -> p (c s)")),
                reads=[f"hT:{c}:{h}" for c in range(NCH) for h in range(2)],
                writes=["dbg"], dma=True)

        sq.barrier()

        if upto < 2:
            block = ctx.enter_context(nc.Block())
            sq.build(block, sems, serial=serial)
            return nc

        with ExitStack() as a2:
            sq.enabled = (upto >= 2)

            def sb2(name, shape, dtp):
                return a2.enter_context(nc.sbuf_tensor(name, shape, dtp, side="right"))

            def ps2(name, shape):
                return a2.enter_context(nc.psum_tensor(name, shape, F32))

            soT = sb2("soT", [128, NCH, SEQ], BF16)
            sqt = sb2("sqt", [128, SEQ], BF16)
            skt = sb2("skt", [128, SEQ], BF16)
            sv = sb2("sv", [128, NT, 130], BF16)
            swq = sb2("swq", [128, 2, NCH, 128], BF16)
            swk = sb2("swk", [128, 2, NCH, 128], BF16)
            swv = sb2("swv", [128, 2, NCH, 128], BF16)
            swo = sb2("swo", [128, NCH, D], BF16)
            sPt = sb2("sPt", [128, 4, 512], BF16)
            sE = sb2("sE", [128, NH, 256], BF16)
            rcol4 = sb2("rcol4", [128, 4], F32)
            p_s = [ps2(f"p_s{i}", [128, 512]) for i in range(4)]
            p_o = [ps2(f"p_o{i}", [128, 4, 66]) for i in range(2)]
            p_qkv = [ps2(f"p_qkv{i}", [128, 512]) for i in range(2)]
            sv4 = sv[:].rearrange("p t (g c) -> p t g c", c=65)

            for c in range(NCH):
                sq.add("gpsimd", lambda e, c=c: e.dma_start(
                    out=swo[:, c, :], in_=wo[128 * c:128 * (c + 1), :]),
                    writes=["swo"], dma=True)
            for h in range(NH):
                esrc = AP(dtau, 384 * h + 127, [[NH * 384 - 1, 128], [1, 256]])
                sq.add("sync", lambda e, h=h, esrc=esrc: e.dma_start(
                    out=sE[:, h, :], in_=esrc),
                    reads=[f"dtau:{h}"], writes=[f"E:{h}"], dma=True)

            for d in range(NP):
                wb = d % 2
                for wname, wt, stile in (("wq", wq, swq), ("wk", wk, swk),
                                         ("wv", wv, swv)):
                    for c in range(NCH):
                        sq.add("gpsimd", lambda e, wt=wt, stile=stile, d=d,
                               wb=wb, c=c: e.dma_start(
                                   out=stile[:, wb, c, :],
                                   in_=wt[128 * c:128 * (c + 1),
                                          128 * d:128 * (d + 1)]),
                               writes=[f"{wname}:{wb}"], dma=True)

                for which, wtag, stile, dst, dsttag, bc, btag in (
                        (0, "wq", swq, sqt, "sqt", sbqc, "bqc"),
                        (1, "wk", swk, skt, "skt", sbkc, "bkc")):
                    for qc in range(NQC):
                        pb = p_qkv[qc % 2]
                        pt = f"pq:{qc % 2}"
                        for c in range(NCH):
                            sq.add("tensor", lambda e, stile=stile, wb=wb, c=c,
                                   qc=qc, pb=pb: e.matmul(
                                       pb[:], stile[:, wb, c, :],
                                       shT[:, c, 512 * qc:512 * (qc + 1)],
                                       start=(c == 0), stop=(c == NCH - 1)),
                                   reads=[f"{wtag}:{wb}", f"hT:{c}:{qc // 2}"],
                                   writes=[pt])
                        sq.add("scalar", lambda e, dst=dst, qc=qc, pb=pb, bc=bc,
                               d=d: e.activation(
                                   dst[:, 512 * qc:512 * (qc + 1)], pb[:],
                                   AF.Identity, bias=bc[:, d:d + 1], scale=F1),
                               reads=[pt, btag], writes=[dsttag])

                sq.add("vector", lambda e: e.memset(sv4[:, :, :, 64:65], 1.0),
                       writes=["svones"])
                for t in range(NT):
                    pb = p_qkv[t % 2]
                    pt = f"pq:{t % 2}"
                    for c in range(NCH):
                        sq.add("tensor", lambda e, wb=wb, c=c, t=t, pb=pb: e.matmul(
                            pb[:, 0:128], shT[:, c, 128 * t:128 * (t + 1)],
                            swv[:, wb, c, :], start=(c == 0), stop=False),
                            reads=[f"wv:{wb}", f"hT:{c}:{t // 8}"], writes=[pt])
                    sq.add("tensor", lambda e, d=d, pb=pb: e.matmul(
                        pb[:, 0:128], ones1[:], sbvb[:, 128 * d:128 * (d + 1)],
                        start=False, stop=True),
                        reads=["bvb", "ones1"], writes=[pt])
                    sq.add("scalar", lambda e, t=t, pb=pb: e.activation(
                        sv4[:, t, :, 0:64],
                        pb[:, 0:128].rearrange("p (g c) -> p g c", c=64),
                        AF.Identity, bias=F0, scale=F1),
                        reads=[pt], writes=[f"sv:{t}"])

                if debug and d == 0:
                    sq.add("sync", lambda e: e.dma_start(out=dbg_qt[:], in_=sqt[:]),
                           reads=["sqt"], writes=["dbg"], dma=True)
                    sq.add("sync", lambda e: e.dma_start(out=dbg_kt[:], in_=skt[:]),
                           reads=["skt"], writes=["dbg"], dma=True)
                    sq.add("sync", lambda e: e.dma_start(
                        out=dbg_v[:], in_=sv[:].rearrange("p t g -> p (t g)")),
                        reads=[f"sv:{t}" for t in range(NT)] + ["svones"],
                        writes=["dbg"], dma=True)
                    sq.add("sync", lambda e: e.dma_start(
                        out=dbg_E[:], in_=sE[:, 0:2, :].rearrange("p h w -> p (h w)")),
                        reads=["E:0", "E:1"], writes=["dbg"], dma=True)

                sobf = soT[:].rearrange("p a b -> p (a b)")
                for hb in range(2):
                    o = 64 * hb
                    h = 2 * d + hb
                    for qc in range(NQC):
                        nkt = 4 * qc + 4
                        par = (2 * d + hb) % 2
                        pob = p_o[par]
                        pot = f"po:{par}"

                        def emit_qk(kt, qc=qc, o=o):
                            qoff = max(0, 128 * kt - 512 * qc)
                            pb = p_s[kt % 4]
                            sq.add("tensor", lambda e, kt=kt, qoff=qoff, pb=pb,
                                   o=o, qc=qc: e.matmul(
                                       pb[:, qoff:512],
                                       skt[o:o + 64, 128 * kt:128 * (kt + 1)],
                                       sqt[o:o + 64, 512 * qc + qoff:512 * (qc + 1)],
                                       start=True, stop=True, tile_position=(o, 0)),
                                   reads=["sqt", "skt"], writes=[f"ps:{kt % 4}"])

                        def emit_sm(kt, qc=qc, h=h):
                            qoff = max(0, 128 * kt - 512 * qc)
                            pb = p_s[kt % 4]
                            sq.add("scalar", lambda e, kt=kt, qoff=qoff, pb=pb: e.activation(
                                sPt[:, kt % 4, qoff:512], pb[:, qoff:512],
                                AF.Exp, bias=EXPB, scale=F1),
                                reads=[f"ps:{kt % 4}"], writes=[f"pt:{kt % 4}"])
                            j = kt - 4 * qc
                            if j >= 0:
                                w = min(240, 512 - 128 * j)
                                sq.add("vector", lambda e, kt=kt, j=j, w=w, h=h: e.tensor_mul(
                                    sPt[:, kt % 4, 128 * j:128 * j + w],
                                    sPt[:, kt % 4, 128 * j:128 * j + w],
                                    sE[:, h, 0:w]),
                                    reads=[f"E:{h}"], writes=[f"pt:{kt % 4}"])
                            elif j == -1:
                                sq.add("vector", lambda e, kt=kt, h=h: e.tensor_mul(
                                    sPt[:, kt % 4, 0:112], sPt[:, kt % 4, 0:112],
                                    sE[:, h, 128:240]),
                                    reads=[f"E:{h}"], writes=[f"pt:{kt % 4}"])

                        def emit_pv(kt, qc=qc, hb=hb, pob=pob, pot=pot):
                            j = kt - 4 * qc
                            for qs in range(max(0, j), 4):
                                sq.add("tensor", lambda e, kt=kt, qs=qs, hb=hb,
                                       qc=qc, pob=pob: e.matmul(
                                           pob[:, qs, 0:65],
                                           sPt[:, kt % 4, 128 * qs:128 * (qs + 1)],
                                           sv[:, kt, 65 * hb:65 * hb + 65],
                                           start=(kt == 0 and qs == 0),
                                           stop=(kt == 4 * qc + qs)),
                                       reads=[f"pt:{kt % 4}", f"sv:{kt}", "svones"],
                                       writes=[pot])

                        L = 2
                        for kt in range(min(L, nkt)):
                            emit_qk(kt)
                        for kt in range(nkt):
                            if kt + L < nkt:
                                emit_qk(kt + L)
                            emit_sm(kt)
                            emit_pv(kt)

                        sq.add("vector", lambda e, pob=pob: e.reciprocal(
                            rcol4[:], pob[:, :, 64]),
                            reads=[pot], writes=["rcol4"])
                        for qs in range(4):
                            t = 4 * qc + qs
                            off = 1024 * t + 64 * h
                            sq.add("vector", lambda e, qs=qs, off=off, pob=pob,
                                   sobf=sobf: e.tensor_scalar(
                                       sobf[:, off:off + 64], pob[:, qs, 0:64],
                                       rcol4[:, qs:qs + 1], None, ALU.mult),
                                reads=[pot, "rcol4"], writes=["sob"])

            if debug:
                sq.add("sync", lambda e: e.dma_start(
                    out=dbg_oT[:], in_=soT[:].rearrange("p c s -> p (c s)")),
                    reads=["sob"], writes=["dbg"], dma=True)

            sobf2 = soT[:].rearrange("p a b -> p (a b)")
            for t in range(NT):
                sq.add("sync", lambda e, t=t, sobf2=sobf2: e.dma_start(
                    out=dy[128 * t:128 * (t + 1), :],
                    in_=sobf2[:, 1024 * t:1024 * (t + 1)]),
                    reads=["sob"], writes=[f"dy:{t}"], dma=True)
            for half in range(2):
                tdeps = [f"dy:{t}" for t in range(8 * half, 8 * half + 8)]
                for c in range(NCH):
                    sq.add("sync", lambda e, half=half, c=c: e.dma_start(
                        out=shT[:, c, 1024 * half:1024 * (half + 1)],
                        in_=dy[1024 * half:1024 * (half + 1), 128 * c:128 * (c + 1)],
                        transpose=True),
                        reads=tdeps, writes=[f"hT:{c}:{half}"], dma=True)

            for t in range(NT):
                for half in range(2):
                    pb = p_qkv[(2 * t + half) % 2]
                    pt = f"pq:{(2 * t + half) % 2}"
                    for c in range(NCH):
                        sq.add("tensor", lambda e, t=t, half=half, c=c, pb=pb: e.matmul(
                            pb[:], shT[:, c, 128 * t:128 * (t + 1)],
                            swo[:, c, 512 * half:512 * (half + 1)],
                            start=(c == 0), stop=False),
                            reads=[f"hT:{c}:{t // 8}", "swo"], writes=[pt])
                    sq.add("tensor", lambda e, half=half, pb=pb: e.matmul(
                        pb[:], ones1[:], sbob[:, 512 * half:512 * (half + 1)],
                        start=False, stop=True),
                        reads=["bob", "ones1"], writes=[pt])
                    sq.add("vector", lambda e, half=half, pb=pb: e.scalar_tensor_tensor(
                        stmp[:], pb[:], 1.0, gat[:, 512 * half:512 * (half + 1)],
                        ALU.mult, ALU.mult),
                        reads=[pt, "gat"], writes=["stmp"])
                    sq.add("vector", lambda e, t=t, half=half: e.tensor_add(
                        sx[:, t, 512 * half:512 * (half + 1)],
                        sx[:, t, 512 * half:512 * (half + 1)], stmp[:]),
                        reads=["stmp"], writes=[f"x:{t}"])

        if debug:
            sq.add("sync", lambda e: e.dma_start(
                out=dbg_xa[:], in_=sx[:].rearrange("p t d2 -> p (t d2)")),
                reads=XT, writes=["dbg"], dma=True)

        sq.barrier()

        with ExitStack() as a3:
            sq.enabled = (upto >= 3)

            def sb3(name, shape, dtp):
                return a3.enter_context(nc.sbuf_tensor(name, shape, dtp, side="right"))

            def ps3(name, shape):
                return a3.enter_context(nc.psum_tensor(name, shape, F32))

            spk = sb3("spk", [128, NCH, 256], I32)
            swf = sb3("swf", [128, NCH, D], BF16)
            sy3 = sb3("sy3", [128, 2, D], BF16)
            sh2 = sb3("sh2", [128, 2, D], BF16)
            st32 = sb3("st32", [128, D], F32)
            sqi = sb3("sqi", [128, D], I32)
            sei = sb3("sei", [128, 256], I32)
            p_f = [ps3(f"p_f{i}", [128, 512]) for i in range(2)]

            for c in range(NCH):
                sq.add("sync", lambda e, c=c: e.dma_start(
                    out=spk[:, c, :], in_=pkt[128 * c:128 * (c + 1), :]),
                    writes=["spk"], dma=True)
            for c in range(NCH):
                for s in range(4):
                    sq.add("vector", lambda e, c=c, s=s: e.tensor_scalar(
                        sei[:], spk[:, c, :], 2 * s, 3,
                        ALU.logical_shift_right, ALU.bitwise_and),
                        reads=["spk"], writes=["sei"])
                    sq.add("vector", lambda e, c=c, s=s: e.tensor_scalar(
                        swf[:, c, 256 * s:256 * (s + 1)], sei[:], 1, None,
                        ALU.subtract),
                        reads=["sei"], writes=[f"swf:{c}"])
                sq.add("vector", lambda e, c=c: e.tensor_mul(
                    swf[:, c, :], swf[:, c, :], gbw[:]),
                    reads=["gbw"], writes=[f"swf:{c}"])

            for t in range(NT):
                b = t % 2
                ln_tile(t, sy3[:, b, :], f"sy:{b}")
                sq.add("vector", lambda e, b=b: e.tensor_mul(
                    sh2[:, b, :], sy3[:, b, :], a2t[:]),
                    reads=[f"sy:{b}", "a2t"], writes=[f"sh2:{b}"])
                sq.add("vector", lambda e, b=b: e.tensor_add(
                    sh2[:, b, :], sh2[:, b, :], b2t[:]),
                    reads=["b2t"], writes=[f"sh2:{b}"])
                sq.add("vector", lambda e, b=b: e.reduce_max(
                    stmp1[:], sh2[:, b, :], AXX, apply_absolute_value=True),
                    reads=[f"sh2:{b}"], writes=["t1"])
                sq.add("vector", lambda e: e.tensor_scalar_max(stmp1[:], stmp1[:], 1e-5),
                       reads=["t1"], writes=["t1"])
                sq.add("vector", lambda e, t=t: e.tensor_scalar(
                    skap[:, t:t + 1], stmp1[:], s_swb[:], 1.0 / 127.0,
                    ALU.mult, ALU.mult),
                    reads=["t1", "swb"], writes=["kap"])
                sq.add("vector", lambda e: e.reciprocal(stmp1b[:], stmp1[:]),
                       reads=["t1"], writes=["t1b"])
                sq.add("vector", lambda e, t=t: e.tensor_scalar_mul(
                    ssc[:, t:t + 1], stmp1b[:], 127.0),
                    reads=["t1b"], writes=["qsc"])
                sq.add("scalar", lambda e, t=t, b=b: e.activation(
                    st32[:], sh2[:, b, :], AF.Identity, scale=ssc[:, t:t + 1],
                    bias=F0),
                    reads=[f"sh2:{b}", "qsc"], writes=["st32"])
                sq.add("vector", lambda e: e.tensor_copy(sqi[:], st32[:]),
                       reads=["st32"], writes=["sqi"])
                sq.add("vector", lambda e, b=b: e.tensor_copy(sy3[:, b, :], sqi[:]),
                       reads=["sqi"], writes=[f"sy:{b}"])
                sq.add("sync", lambda e, t=t, b=b: e.dma_start(
                    out=dy[128 * t:128 * (t + 1), :], in_=sy3[:, b, :]),
                    reads=[f"sy:{b}"], writes=[f"dy:{t}"], dma=True)
            for half in range(2):
                tdeps = [f"dy:{t}" for t in range(8 * half, 8 * half + 8)]
                for c in range(NCH):
                    sq.add("sync", lambda e, half=half, c=c: e.dma_start(
                        out=shT[:, c, 1024 * half:1024 * (half + 1)],
                        in_=dy[1024 * half:1024 * (half + 1), 128 * c:128 * (c + 1)],
                        transpose=True),
                        reads=tdeps, writes=[f"hT:{c}:{half}"], dma=True)

            if debug:
                sq.add("sync", lambda e: e.dma_start(
                    out=dbg_wf[:], in_=swf[:].rearrange("p c d2 -> p (c d2)")),
                    reads=[f"swf:{c}" for c in range(NCH)], writes=["dbg"], dma=True)
                sq.add("sync", lambda e: e.dma_start(out=dbg_q[:], in_=dy[:]),
                       reads=[f"dy:{t}" for t in range(NT)], writes=["dbg"], dma=True)
                sq.add("sync", lambda e: e.dma_start(
                    out=dbg_kap[:, 0:NT], in_=skap[:]),
                    reads=["kap"], writes=["dbg"], dma=True)
                sq.add("sync", lambda e: e.dma_start(
                    out=dbg_kap[:, NT:2 * NT], in_=ssc[:]),
                    reads=["qsc"], writes=["dbg"], dma=True)

            for t in range(NT):
                for half in range(2):
                    pb = p_f[(2 * t + half) % 2]
                    pt = f"pf:{(2 * t + half) % 2}"
                    for c in range(NCH):
                        sq.add("tensor", lambda e, t=t, half=half, c=c, pb=pb: e.matmul(
                            pb[:], shT[:, c, 128 * t:128 * (t + 1)],
                            swf[:, c, 512 * half:512 * (half + 1)],
                            start=(c == 0), stop=(c == NCH - 1)),
                            reads=[f"hT:{c}:{t // 8}", f"swf:{c}"], writes=[pt]),
                    sq.add("vector", lambda e, t=t, half=half, pb=pb: e.scalar_tensor_tensor(
                        stmp[:], pb[:], skap[:, t:t + 1],
                        gbiast[:, 512 * half:512 * (half + 1)],
                        ALU.mult, ALU.add),
                        reads=[pt, "kap", "gbiast"], writes=["stmp"])
                    sq.add("vector", lambda e, t=t, half=half: e.tensor_add(
                        sx[:, t, 512 * half:512 * (half + 1)],
                        sx[:, t, 512 * half:512 * (half + 1)], stmp[:]),
                        reads=["stmp"], writes=[f"x:{t}"])

            for t in range(NT):
                sq.add("sync", lambda e, t=t: e.dma_start(
                    out=out[128 * t:128 * (t + 1), :], in_=sx[:, t, :]),
                    reads=[f"x:{t}"], writes=["outdma"], dma=True)
            sq.add("vector", lambda e: e.engine_nop(), reads=["outdma"],
                   writes=["done"])

        block = ctx.enter_context(nc.Block())
        sq.build(block, sems, serial=serial)
    return nc


_CACHED = {}


def _get_nc():
    if "nc" not in _CACHED:
        _CACHED["nc"] = build_nc()
    return _CACHED["nc"]


def kernel(L0_attn_x, L0_attn_Wq, L0_attn_bq, L0_attn_Wk, L0_attn_bk,
           L0_attn_Wv, L0_attn_bv, L0_attn_Wo, L0_attn_bo, L0_attn_rel_emb,
           L0_ffn_packed_weight, L0_ffn_weight_scale, L0_ffn_bias, L0_c,
           L0_wmod):
    x = np.ascontiguousarray(np.asarray(L0_attn_x, np.float32))
    wqT = np.ascontiguousarray(np.asarray(L0_attn_Wq, np.float32).T)
    wkT = np.ascontiguousarray(np.asarray(L0_attn_Wk, np.float32).T)
    wvT = np.ascontiguousarray(np.asarray(L0_attn_Wv, np.float32).T)
    woT = np.ascontiguousarray(np.asarray(L0_attn_Wo, np.float32).T)
    wmv = np.ascontiguousarray(np.asarray(L0_wmod, np.float32))
    pkT = np.ascontiguousarray(
        np.asarray(L0_ffn_packed_weight, np.int32).reshape(256, 1024).T)
    relv = np.ascontiguousarray(np.asarray(L0_attn_rel_emb, np.float32))
    ohv = _oh_const()
    c = np.asarray(L0_c, np.float32)
    bqcv = np.ascontiguousarray(np.asarray(L0_attn_bq, np.float32).reshape(8, 128).T)
    bkcv = np.ascontiguousarray(np.asarray(L0_attn_bk, np.float32).reshape(8, 128).T)
    bvr = np.ascontiguousarray(np.asarray(L0_attn_bv, np.float32).reshape(1, D))
    borv = np.ascontiguousarray(np.asarray(L0_attn_bo, np.float32).reshape(1, D))
    fbrv = np.ascontiguousarray(np.asarray(L0_ffn_bias, np.float32).reshape(1, D))
    wscv = np.asarray(L0_ffn_weight_scale, np.float32).reshape(1, 1)
    eye16 = np.eye(16, dtype=np.float32)

    nc = _get_nc()
    in_maps = []
    for b in range(8):
        in_maps.append({
            "x": x[b], "wq": wqT, "wk": wkT, "wv": wvT, "wo": woT,
            "wm": wmv, "pk": pkT,
            "ccol": np.ascontiguousarray(c[b].reshape(8, 128).T),
            "bqc": bqcv, "bkc": bkcv, "bvr": bvr, "bor": borv, "fbr": fbrv,
            "rel": relv, "oh": ohv, "wsc": wscv, "idf": eye16,
        })
    res = run_bass_kernel_spmd(nc, in_maps, core_ids=list(range(8)))
    return np.stack([r["out"] for r in res.results], axis=0)


# version 07
# speedup vs baseline: 1.3090x; 1.0468x over previous
import math
from contextlib import ExitStack

import ml_dtypes
import numpy as np

import concourse.bass as bass
import concourse.mybir as mybir
from concourse.ap import AP
from concourse.bass_utils import run_bass_kernel_spmd

F32 = mybir.dt.float32
F32R = mybir.dt.float32r
BF16 = mybir.dt.bfloat16
I32 = mybir.dt.int32
AF = mybir.ActivationFunctionType
ALU = mybir.AluOpType
AXX = mybir.AxisListType.X

SEQ = 2048
D = 1024
NH = 16
NT = SEQ // 128
NCH = D // 128
NP = NH // 2
NQC = SEQ // 512
EXPB = np.float32(-12.0)
F1 = np.float32(1.0)
F0 = np.float32(0.0)


class Sched:
    ENGINES = ("tensor", "vector", "scalar", "gpsimd", "sync")
    NDMA = 8
    BARRIER_DMA = ("sync",)

    def __init__(self):
        self.steps = []

    enabled = True

    def add(self, eng, fn, reads=(), writes=(), dma=False):
        if not self.enabled:
            return
        self.steps.append([eng, fn, tuple(reads), tuple(writes), bool(dma)])

    def barrier(self):
        self.steps.append(["BARRIER", None, (), (), False])

    def build(self, block, sems, serial=False):
        steps = self.steps
        n = len(steps)
        cpos, dpos = {}, {}
        ccount = {e: 0 for e in self.ENGINES}
        dcount = {e: 0 for e in self.ENGINES}
        deps = [set() for _ in range(n)]
        last_w, readers = {}, {}
        dqueues = {}
        bar_after = [None] * n
        snap = None
        for i, (eng, fn, rd, wr, dma) in enumerate(steps):
            if eng == "BARRIER":
                snap = (dict(ccount), dict(dcount))
                continue
            bar_after[i] = snap
            if dma:
                dpos[i] = dcount[eng]
                dcount[eng] += 1
            else:
                cpos[i] = ccount[eng]
                ccount[eng] += 1
            d = deps[i]
            for r in rd:
                j = last_w.get(r)
                if j is not None:
                    d.add(j)
            for w in wr:
                j = last_w.get(w)
                if j is not None:
                    d.add(j)
                for j in readers.get(w, ()):
                    d.add(j)
            if dma:
                q = dqueues.setdefault(eng, [])
                if len(q) >= self.NDMA:
                    d.add(q[-self.NDMA])
                q.append(i)
            for w in wr:
                last_w[w] = i
                readers[w] = []
            for r in rd:
                readers.setdefault(r, []).append(i)

        sched = self

        def mk(engine_name):
            def emit(eng):
                watermark = {}

                def wait_val(key, sem, val):
                    if watermark.get(key, -1) >= val:
                        return
                    watermark[key] = val
                    eng.wait_ge(sem, val)

                def wait_step(j):
                    e2, _, _, _, dm2 = steps[j]
                    if dm2:
                        k = dpos[j]
                        key = f"d:{e2}:{k % sched.NDMA}"
                        wait_val(key, sems[key], 16 * (k // sched.NDMA + 1))
                    elif e2 != engine_name or e2 != "tensor":
                        key = f"c:{e2}"
                        wait_val(key, sems[key], cpos[j] + 1)

                emitted_bar = None
                run_cc = {e2: 0 for e2 in sched.ENGINES}
                run_dc = {e2: 0 for e2 in sched.ENGINES}
                for i, (e, fn, rd, wr, dma) in enumerate(steps):
                    if e == "BARRIER":
                        continue
                    if serial and e != engine_name:
                        if dma:
                            run_dc[e] += 1
                        else:
                            run_cc[e] += 1
                        continue
                    if serial:
                        for e2 in sched.ENGINES:
                            if run_cc[e2] > 0:
                                wait_val(f"c:{e2}", sems[f"c:{e2}"], run_cc[e2])
                            for kk in range(max(0, run_dc[e2] - sched.NDMA), run_dc[e2]):
                                key = f"d:{e2}:{kk % sched.NDMA}"
                                wait_val(key, sems[key],
                                         16 * (kk // sched.NDMA + 1))
                        if dma:
                            run_dc[e] += 1
                        else:
                            run_cc[e] += 1
                        ins = fn(eng)
                        if dma:
                            k = dpos[i]
                            ins.then_inc(sems[f"d:{engine_name}:{k % sched.NDMA}"], 16)
                        else:
                            ins.then_inc(sems[f"c:{engine_name}"], 1)
                        continue
                    if e != engine_name:
                        continue
                    if bar_after[i] is not None and bar_after[i] is not emitted_bar:
                        cc, dc = bar_after[i]
                        emitted_bar = bar_after[i]
                        for e2 in sched.ENGINES:
                            if e2 != engine_name and cc[e2] > 0:
                                wait_val(f"c:{e2}", sems[f"c:{e2}"], cc[e2])
                        for e2 in sched.BARRIER_DMA:
                            for kk in range(max(0, dc[e2] - sched.NDMA), dc[e2]):
                                key = f"d:{e2}:{kk % sched.NDMA}"
                                wait_val(key, sems[key],
                                         16 * (kk // sched.NDMA + 1))
                    for j in sorted(deps[i]):
                        wait_step(j)
                    ins = fn(eng)
                    if dma:
                        k = dpos[i]
                        ins.then_inc(sems[f"d:{engine_name}:{k % sched.NDMA}"], 16)
                    else:
                        ins.then_inc(sems[f"c:{engine_name}"], 1)
            return emit

        block.tensor(mk("tensor"))
        block.vector(mk("vector"))
        block.scalar(mk("scalar"))
        block.gpsimd(mk("gpsimd"))
        block.sync(mk("sync"))


def _oh_const():
    d = np.arange(128)
    t = np.log(np.maximum(d, 1).astype(np.float32) / np.float32(16.0))
    t = (t / np.float32(math.log(8.0))).astype(np.float32)
    t = (t * np.float32(16.0)).astype(np.float32)
    vl = np.minimum(16 + t.astype(np.int32), 31)
    bucket = np.where(d < 16, d, vl)
    oh = np.zeros((32, 128), np.float32)
    scale = np.float32((D // NH) ** (-0.5))
    for dlt in range(128):
        oh[bucket[dlt], dlt] = scale
    return oh


def build_nc(serial=False, upto=3):
    nc = bass.Bass("TRN2", target_bir_lowering=False)
    dt = nc.dram_tensor
    x_in = dt("x", [SEQ, D], F32, kind="ExternalInput")
    wq = dt("wq", [D, D], F32, kind="ExternalInput")
    wk = dt("wk", [D, D], F32, kind="ExternalInput")
    wv = dt("wv", [D, D], F32, kind="ExternalInput")
    wo = dt("wo", [D, D], F32, kind="ExternalInput")
    wm = dt("wm", [D, 6 * D], F32, kind="ExternalInput")
    pkt = dt("pk", [D, 256], I32, kind="ExternalInput")
    ccol = dt("ccol", [128, 8], F32, kind="ExternalInput")
    bqc_d = dt("bqc", [128, 8], F32, kind="ExternalInput")
    bkc_d = dt("bkc", [128, 8], F32, kind="ExternalInput")
    bvr = dt("bvr", [1, D], F32, kind="ExternalInput")
    bor = dt("bor", [1, D], F32, kind="ExternalInput")
    fbr = dt("fbr", [1, D], F32, kind="ExternalInput")
    rel = dt("rel", [32, 16], F32, kind="ExternalInput")
    ohd = dt("oh", [32, 128], F32, kind="ExternalInput")
    wsc = dt("wsc", [1, 1], F32, kind="ExternalInput")
    idf = dt("idf", [16, 16], F32, kind="ExternalInput")
    eyed = dt("eye128", [128, 128], BF16, kind="ExternalInput")
    out = dt("out", [SEQ, D], F32, kind="ExternalOutput")
    drows = dt("drows", [1, 8 * D], F32, kind="Internal")
    dsum = dt("dsum", [1, 2048], F32, kind="Internal")
    dsumb = dt("dsumb", [1, 2048], BF16, kind="Internal")
    drow2 = dt("drow2", [1, 4 * D], BF16, kind="Internal")
    dtaus = dt("dtaus", [16, 384], BF16, kind="Internal")
    dtau = dt("dtau", [128, NH * 384], BF16, kind="Internal")

    sq = Sched()
    with ExitStack() as ctx:
        def sb(name, shape, dtp, **kw):
            return ctx.enter_context(nc.sbuf_tensor(name, shape, dtp, **kw))

        sx = sb("sx", [128, NT, D], F32)
        shT = sb("shT", [128, NCH, SEQ], BF16)
        gat = sb("gat", [128, D], BF16)
        swq = sb("swq", [128, 2, NCH, 128], BF16)
        swk = sb("swk", [128, 2, NCH, 128], BF16)
        swv = sb("swv", [128, 2, NCH, 128], BF16)
        swo = sb("swo", [128, NCH, D], BF16)
        sid128 = sb("sid128", [128, 128], BF16)
        sccol = sb("sccol", [128, 8], F32)
        ssil = sb("ssil", [128, 8], F32)
        scsil = sb("scsil", [128, 8], F32R)
        sbqc = sb("sbqc", [128, 8], F32)
        sbkc = sb("sbkc", [128, 8], F32)
        a1c = sb("a1c", [128, 16], F32)
        s_swb = sb("s_swb", [128, 1], F32)
        sbvb = sb("sbvb", [1, D], BF16)
        sbob = sb("sbob", [1, D], BF16)
        swscb = sb("swscb", [1, 1], BF16)
        ones1 = sb("ones1", [1, 128], BF16)
        stats6b = sb("stats6b", [128, NT, 2, 6], F32)
        stats2b = sb("stats2b", [128, NT, 2], F32)
        svar16 = sb("svar16", [128, NT], F32)
        srcol = sb("srcol", [128, NT], F32)
        snmr = sb("snmr", [128, NT], F32)
        skap = sb("skap", [128, NT], F32)
        ssc = sb("ssc", [128, NT], F32)
        stmp1 = sb("stmp1", [128, 1], F32)
        stmp1b = sb("stmp1b", [128, 1], F32)

        sems = {}
        for e in Sched.ENGINES:
            sems[f"c:{e}"] = ctx.enter_context(nc.semaphore(f"c_{e}"))
        for e in ("sync", "gpsimd"):
            for k in range(Sched.NDMA):
                sems[f"d:{e}:{k}"] = ctx.enter_context(nc.semaphore(f"d_{e}_{k}"))

        XT = [f"x:{t}" for t in range(NT)]

        def ln_stats(t):
            sq.add("vector", lambda e, t=t: e.bn_stats(
                stats6b[:, t, 0, :], sx[:, t, 0:512]),
                reads=[f"x:{t}"], writes=[f"st6:{t}"])
            sq.add("vector", lambda e, t=t: e.bn_stats(
                stats6b[:, t, 1, :], sx[:, t, 512:1024]),
                reads=[f"x:{t}"], writes=[f"st6:{t}"])
            sq.add("vector", lambda e, t=t: e.bn_aggr(
                stats2b[:, t, :], stats6b[:, t, :, :]),
                reads=[f"st6:{t}"], writes=[f"st2:{t}"])

        def ln_batch():
            st2 = [f"st2:{t}" for t in range(NT)]
            sq.add("vector", lambda e: e.tensor_scalar_add(
                svar16[:], stats2b[:, :, 1], 1e-5),
                reads=st2, writes=["svar"])
            sq.add("scalar", lambda e: e.activation(
                svar16[:], svar16[:], AF.Sqrt, bias=F0, scale=F1),
                reads=["svar"], writes=["svar"])
            sq.add("vector", lambda e: e.reciprocal(srcol[:], svar16[:]),
                   reads=["svar"], writes=["r"])
            sq.add("vector", lambda e: e.tensor_mul(
                snmr[:], stats2b[:, :, 0], srcol[:]),
                reads=st2 + ["r"], writes=["nmr"])
            sq.add("vector", lambda e: e.tensor_scalar_mul(snmr[:], snmr[:], -1.0),
                   reads=["nmr"], writes=["nmr"])

        def ln_apply(t, sytile, sytag):
            sq.add("scalar", lambda e, t=t, sytile=sytile: e.activation(
                sytile, sx[:, t, :], AF.Identity,
                scale=srcol[:, t:t + 1], bias=snmr[:, t:t + 1]),
                reads=[f"x:{t}", "r", "nmr"], writes=[sytag])

        with ExitStack() as a1:
            def sb1(name, shape, dtp):
                return a1.enter_context(nc.sbuf_tensor(name, shape, dtp, side="right"))

            def ps1(name, shape):
                return a1.enter_context(nc.psum_tensor(name, shape, F32))

            swm = sb1("swm", [128, 2, 2048], F32R)
            sy = sb1("sy", [128, 2, D], BF16)
            srow = sb1("srow", [1, D], F32)
            srow2 = sb1("srow2", [1, D], F32)
            srowb = sb1("srowb", [1, D], BF16)
            srel = sb1("srel", [32, 16], F32R)
            soh = sb1("soh", [32, 128], F32R)
            st5 = sb1("st5", [16, 128], F32)
            stau = sb1("stau", [16, 384], F32)
            staub = sb1("staub", [16, 384], BF16)
            srep = sb1("srep", [128, 384], BF16)
            s16 = sb1("s16", [16, 128], F32)
            sidf = sb1("sidf", [16, 16], F32)
            srowst = sb1("srowst", [1, 2, 512], F32)
            p_mod = ps1("p_mod", [1, 2048])
            p_gen = ps1("p_gen", [128, 512])
            p_c16 = ps1("p_c16", [128, 16])
            p_tr = [a1.enter_context(nc.psum_tensor(f"p_tr{i}", [128, 128], BF16))
                    for i in range(2)]

            for t in range(NT):
                sq.add("sync", lambda e, t=t: e.dma_start(
                    out=sx[:, t, :], in_=x_in[128 * t:128 * (t + 1), :]),
                    writes=[f"x:{t}"], dma=True)
            sq.add("sync", lambda e: e.dma_start(out=sccol[:], in_=ccol[:]),
                   writes=["ccol"], dma=True)
            sq.add("sync", lambda e: e.dma_start(out=sbqc[:], in_=bqc_d[:]),
                   writes=["bqc"], dma=True)
            sq.add("sync", lambda e: e.dma_start(out=sbkc[:], in_=bkc_d[:]),
                   writes=["bkc"], dma=True)
            sq.add("sync", lambda e: e.dma_start(out=srow2[:], in_=bvr[:]),
                   writes=["srow2"], dma=True)
            sq.add("vector", lambda e: e.tensor_copy(sbvb[:], srow2[:]),
                   reads=["srow2"], writes=["bvb"])
            sq.add("sync", lambda e: e.dma_start(out=srow[:], in_=bor[:]),
                   writes=["srow"], dma=True)
            sq.add("vector", lambda e: e.tensor_copy(sbob[:], srow[:]),
                   reads=["srow"], writes=["bob"])
            sq.add("gpsimd", lambda e: e.dma_start(out=srel[:], in_=rel[:]),
                   writes=["rel"], dma=True)
            sq.add("gpsimd", lambda e: e.dma_start(out=soh[:], in_=ohd[:]),
                   writes=["oh"], dma=True)
            sq.add("sync", lambda e: e.dma_start(out=sidf[:], in_=idf[:]),
                   writes=["idf"], dma=True)
            sq.add("sync", lambda e: e.dma_start(out=sid128[:], in_=eyed[:]),
                   writes=["sid"], dma=True)
            sq.add("vector", lambda e: e.memset(ones1[:], 1.0), writes=["ones1"])
            sq.add("sync", lambda e: e.dma_start(out=stmp1[0:1, 0:1], in_=wsc[:]),
                   writes=["wscrow"], dma=True)
            sq.add("vector", lambda e: e.tensor_copy(swscb[:], stmp1[0:1, 0:1]),
                   reads=["wscrow"], writes=["wscb"])
            sq.add("tensor", lambda e: e.matmul(
                p_gen[:, 0:1], ones1[:], swscb[:], start=True, stop=True),
                reads=["wscb", "ones1"], writes=["p_gen"])
            sq.add("vector", lambda e: e.tensor_copy(s_swb[:], p_gen[:, 0:1]),
                   reads=["p_gen"], writes=["swb"])

            sq.add("scalar", lambda e: e.activation(ssil[:], sccol[:], AF.Silu, bias=F0, scale=F1),
                   reads=["ccol"], writes=["sil"])
            sq.add("vector", lambda e: e.tensor_copy(scsil[:], ssil[:]),
                   reads=["sil"], writes=["csil"])

            DR = []
            for jg in range(3):
                for c in range(NCH):
                    p = (NCH * jg + c) % 2
                    sq.add("gpsimd", lambda e, jg=jg, c=c, p=p: e.dma_start(
                        out=swm[:, p, :],
                        in_=wm[128 * c:128 * (c + 1), 2048 * jg:2048 * (jg + 1)]),
                        writes=[f"swm:{p}"], dma=True)
                    for sl in range(4):
                        sq.add("tensor", lambda e, c=c, sl=sl, p=p: e.matmul(
                            p_mod[:, 512 * sl:512 * (sl + 1)],
                            scsil[:, c:c + 1],
                            swm[:, p, 512 * sl:512 * (sl + 1)],
                            start=(c == 0), stop=(c == NCH - 1)),
                            reads=["csil", f"swm:{p}"],
                            writes=[f"p_mod:{sl}"])
                for sl in range(4):
                    sq.add("scalar", lambda e, sl=sl: e.activation(
                        srowst[0:1, sl % 2, :], p_mod[0:1, 512 * sl:512 * (sl + 1)],
                        AF.Identity, bias=F0, scale=F1),
                        reads=[f"p_mod:{sl}"], writes=[f"srowst:{sl % 2}"])
                    sq.add("sync", lambda e, jg=jg, sl=sl: e.dma_start(
                        out=drows[:, 2048 * jg + 512 * sl:2048 * jg + 512 * (sl + 1)],
                        in_=srowst[0:1, sl % 2, :]),
                        reads=[f"srowst:{sl % 2}"], writes=[f"dr:{jg}:{sl}"], dma=True)
                    DR.append(f"dr:{jg}:{sl}")

                if jg == 0:
                    sq.add("sync", lambda e: e.dma_start(
                        out=s16[:],
                        in_=drows[:, 0:2048].rearrange("o (r j) -> (o r) j", j=128)),
                        reads=DR[0:4], writes=["s16"], dma=True)
                    sq.add("tensor", lambda e: e.transpose(p_c16[:], s16[:], sidf[:]),
                           reads=["s16", "idf"], writes=["p_c16"])
                    sq.add("vector", lambda e: e.tensor_copy(a1c[:], p_c16[:]),
                           reads=["p_c16"], writes=["a1c"])
                    sq.add("vector", lambda e: e.tensor_scalar_add(
                        a1c[:, 8:16], a1c[:, 8:16], 1.0),
                        reads=["a1c"], writes=["a1c"])

            for t in range(NT):
                ln_stats(t)
            ln_batch()
            for t in range(NT):
                b = t % 2
                ln_apply(t, sy[:, b, :], f"sy:{b}")
                for c in range(NCH):
                    pp = c % 2
                    sq.add("tensor", lambda e, b=b, c=c, pp=pp: e.transpose(
                        p_tr[pp][:], sy[:, b, 128 * c:128 * (c + 1)], sid128[:]),
                        reads=[f"sy:{b}", "sid"], writes=[f"ptr:{pp}"])
                    sq.add("vector", lambda e, t=t, c=c, pp=pp: e.tensor_scalar(
                        shT[:, c, 128 * t:128 * (t + 1)], p_tr[pp][:],
                        a1c[:, 8 + c:9 + c], a1c[:, c:c + 1],
                        ALU.mult, ALU.add),
                        reads=[f"ptr:{pp}", "a1c"], writes=[f"hT:{c}:{t // 4}"])

            def load_row(offs, deps):
                sq.add("sync", lambda e, offs=offs: e.dma_start(
                    out=srow[:], in_=drows[:, offs:offs + D]),
                    reads=list(deps), writes=["srow"], dma=True)

            def bcast_tile(dst, tag, plus1=False):
                for half in range(2):
                    sq.add("tensor", lambda e, half=half: e.matmul(
                        p_gen[:], ones1[:], srowb[:, 512 * half:512 * (half + 1)],
                        start=True, stop=True),
                        reads=["srowb", "ones1"], writes=["p_gen"])
                    bias = F1 if plus1 else F0
                    sq.add("scalar", lambda e, half=half, dst=dst, bias=bias: e.activation(
                        dst[:, 512 * half:512 * (half + 1)], p_gen[:], AF.Identity,
                        bias=bias, scale=F1),
                        reads=["p_gen"], writes=[tag])

            def clamp_gate(offs, deps):
                load_row(offs, deps)
                sq.add("scalar", lambda e: e.activation(srow2[:], srow[:], AF.Abs, bias=F0, scale=F1),
                       reads=["srow"], writes=["srow2"])
                sq.add("vector", lambda e: e.tensor_scalar_max(srow2[:], srow2[:], 0.25),
                       reads=["srow2"], writes=["srow2"])
                sq.add("vector", lambda e: e.tensor_scalar(
                    srow[:], srow[:], 0.0, None, ALU.is_ge),
                    reads=["srow"], writes=["srow"])
                sq.add("vector", lambda e: e.tensor_scalar(
                    srow[:], srow[:], 2.0, -1.0,
                    ALU.mult, ALU.add),
                    reads=["srow"], writes=["srow"])
                sq.add("vector", lambda e: e.tensor_mul(srow2[:], srow2[:], srow[:]),
                       reads=["srow"], writes=["srow2"])

            def save_row(slot):
                sq.add("sync", lambda e, slot=slot: e.dma_start(
                    out=drow2[:, D * slot:D * (slot + 1)], in_=srowb[:]),
                    reads=["srowb"], writes=[f"dr2:{slot}"], dma=True)

            clamp_gate(2048, ["dr:1:0", "dr:1:1"])
            sq.add("vector", lambda e: e.tensor_copy(srowb[:], srow2[:]),
                   reads=["srow2"], writes=["srowb"])
            bcast_tile(gat, "gat")
            clamp_gate(5120, ["dr:2:2", "dr:2:3"])
            sq.add("vector", lambda e: e.tensor_copy(srowb[:], srow2[:]),
                   reads=["srow2"], writes=["srowb"])
            save_row(0)
            sq.add("sync", lambda e: e.dma_start(out=srow[:], in_=fbr[:]),
                   writes=["srow"], dma=True)
            sq.add("vector", lambda e: e.tensor_mul(srow2[:], srow2[:], srow[:]),
                   reads=["srow"], writes=["srow2"])
            sq.add("vector", lambda e: e.tensor_copy(srowb[:], srow2[:]),
                   reads=["srow2"], writes=["srowb"])
            save_row(1)
            load_row(4096, ["dr:2:0", "dr:2:1"])
            sq.add("vector", lambda e: e.tensor_copy(srowb[:], srow[:]),
                   reads=["srow"], writes=["srowb"])
            save_row(2)
            load_row(3072, ["dr:1:2", "dr:1:3"])
            sq.add("vector", lambda e: e.tensor_copy(srowb[:], srow[:]),
                   reads=["srow"], writes=["srowb"])
            save_row(3)

            sq.add("tensor", lambda e: e.matmul(p_gen[0:16, 0:128], srel[:], soh[:],
                                                start=True, stop=True),
                   reads=["rel", "oh"], writes=["p_gen"])
            sq.add("vector", lambda e: e.tensor_copy(st5[:], p_gen[0:16, 0:128]),
                   reads=["p_gen"], writes=["t5"])
            sq.add("vector", lambda e: e.tensor_scalar(
                st5[:], st5[:], st5[:, 127:128], None, ALU.subtract),
                reads=["t5"], writes=["t5"])
            sq.add("vector", lambda e: e.memset(stau[:, 0:127], -1e30), writes=["tau"])
            sq.add("vector", lambda e: e.memset(stau[:, 240:384], 0.0), writes=["tau"])
            sq.add("vector", lambda e: e.tensor_copy(stau[:, 127:240], st5[:, 0:113]),
                   reads=["t5"], writes=["tau"])
            sq.add("scalar", lambda e: e.activation(stau[:], stau[:], AF.Exp, bias=F0, scale=F1),
                   reads=["tau"], writes=["tau"])
            sq.add("vector", lambda e: e.tensor_copy(staub[:], stau[:]),
                   reads=["tau"], writes=["taub"])
            sq.add("sync", lambda e: e.dma_start(out=dtaus[:], in_=staub[:]),
                   reads=["taub"], writes=["dtaus"], dma=True)
            for h in range(NH):
                sq.add("sync", lambda e, h=h: e.dma_start(
                    out=srowb[:, 0:384], in_=dtaus[h:h + 1, :]),
                    reads=["dtaus"], writes=["srowb"], dma=True)
                sq.add("tensor", lambda e: e.matmul(
                    p_gen[:, 0:384], ones1[:], srowb[:, 0:384], start=True, stop=True),
                    reads=["srowb", "ones1"], writes=["p_gen"])
                sq.add("vector", lambda e: e.tensor_copy(srep[:], p_gen[:, 0:384]),
                       reads=["p_gen"], writes=["srep"])
                sq.add("sync", lambda e, h=h: e.dma_start(
                    out=dtau[:, 384 * h:384 * (h + 1)], in_=srep[:]),
                    reads=["srep"], writes=[f"dtau:{h}"], dma=True)

            def wload(d):
                wb = d % 2
                for wname, wt, stile in (("wq", wq, swq), ("wk", wk, swk),
                                         ("wv", wv, swv)):
                    for c in range(NCH):
                        sq.add("gpsimd", lambda e, wt=wt, stile=stile, d=d,
                               wb=wb, c=c: e.dma_start(
                                   out=stile[:, wb, c, :],
                                   in_=wt[128 * c:128 * (c + 1),
                                          128 * d:128 * (d + 1)]),
                               writes=[f"{wname}:{wb}"], dma=True)

            wload(0)
            wload(1)
            for c in range(NCH):
                sq.add("gpsimd", lambda e, c=c: e.dma_start(
                    out=swo[:, c, :], in_=wo[128 * c:128 * (c + 1), :]),
                    writes=["swo"], dma=True)

        sq.barrier()

        if upto < 2:
            block = ctx.enter_context(nc.Block())
            sq.build(block, sems, serial=serial)
            return nc

        with ExitStack() as a2:
            sq.enabled = (upto >= 2)

            def sb2(name, shape, dtp):
                return a2.enter_context(nc.sbuf_tensor(name, shape, dtp, side="right"))

            def ps2(name, shape):
                return a2.enter_context(nc.psum_tensor(name, shape, F32))

            soT = sb2("soT", [128, NCH, SEQ], BF16)
            sqt = sb2("sqt", [128, SEQ], BF16)
            skt = sb2("skt", [128, SEQ], BF16)
            sv = sb2("sv", [128, NT, 130], BF16)
            sPt = sb2("sPt", [128, 4, 512], BF16)
            sE = sb2("sE", [128, NH, 240], BF16)
            srows1 = sb2("srows1", [1, 4, 512], F32)
            srs16 = sb2("srs16", [16, 128], F32)
            srs16b = sb2("srs16b", [16, 128], BF16)
            srecb4 = sb2("srecb4", [1, 2048], BF16)
            p_oacc = ps2("p_oacc", [65, SEQ])
            p_s = [ps2(f"p_s{i}", [128, 512]) for i in range(4)]
            sv4 = sv[:].rearrange("p t (g c) -> p t g c", c=65)

            def epi_A():
                sq.add("sync", lambda e: e.dma_start(
                    out=srs16[:],
                    in_=dsum[:].rearrange("o (p j) -> (o p) j", p=16)),
                    reads=["dsum"], writes=["srs16"], dma=True)
                sq.add("vector", lambda e: e.reciprocal(srs16[:], srs16[:]),
                       reads=["srs16"], writes=["srs16"])
                sq.add("vector", lambda e: e.tensor_copy(srs16b[:], srs16[:]),
                       reads=["srs16"], writes=["srs16b"])
                sq.add("sync", lambda e: e.dma_start(
                    out=dsumb[:].rearrange("o (p j) -> (o p) j", p=16),
                    in_=srs16b[:]),
                    reads=["srs16b"], writes=["dsumb"], dma=True)
                sq.add("sync", lambda e: e.dma_start(
                    out=srecb4[:], in_=dsumb[:]),
                    reads=["dsumb"], writes=["srecb4"], dma=True)

            def epi_B(d0, o0):
                for qc in range(NQC):
                    pb = p_s[qc % 4]
                    pt = f"ps:{qc % 4}"
                    sq.add("tensor", lambda e, qc=qc, pb=pb: e.matmul(
                        pb[0:64, :], ones1[:, 0:64],
                        srecb4[0:1, 512 * qc:512 * (qc + 1)],
                        start=True, stop=True),
                        reads=["srecb4"], writes=[pt])
                    sq.add("vector", lambda e, qc=qc, o=o0, d=d0, pb=pb,
                           pt=pt: e.tensor_mul(
                        soT[o:o + 64, d, 512 * qc:512 * (qc + 1)],
                        soT[o:o + 64, d, 512 * qc:512 * (qc + 1)],
                        pb[0:64, :]),
                        reads=[pt, "soT"], writes=["soT"])

            epi = {"a": None, "b": None}

            for h in range(NH):
                esrc = AP(dtau, 384 * h + 127, [[NH * 384 - 1, 128], [1, 240]])
                sq.add("sync", lambda e, h=h, esrc=esrc: e.dma_start(
                    out=sE[:, h, :], in_=esrc),
                    reads=[f"dtau:{h}"], writes=[f"E:{h}"], dma=True)

            for d in range(NP):
                wb = d % 2
                for which, wtag, stile, dst, dsttag, bc, btag in (
                        (0, "wq", swq, sqt, "sqt", sbqc, "bqc"),
                        (1, "wk", swk, skt, "skt", sbkc, "bkc")):
                    for qc in range(NQC):
                        pb = p_s[2 * which + (qc % 2)]
                        pt = f"ps:{2 * which + (qc % 2)}"
                        for c in range(NCH):
                            sq.add("tensor", lambda e, stile=stile, wb=wb, c=c,
                                   qc=qc, pb=pb: e.matmul(
                                       pb[:], stile[:, wb, c, :],
                                       shT[:, c, 512 * qc:512 * (qc + 1)],
                                       start=(c == 0), stop=(c == NCH - 1)),
                                   reads=[f"{wtag}:{wb}", f"hT:{c}:{qc}"],
                                   writes=[pt])
                        sq.add("vector", lambda e, dst=dst, qc=qc, pb=pb, bc=bc,
                               d=d, pt=pt: e.tensor_scalar(
                                   dst[:, 512 * qc:512 * (qc + 1)], pb[:],
                                   bc[:, d:d + 1], None, ALU.add),
                               reads=[pt, btag], writes=[dsttag])

                sq.add("vector", lambda e: e.memset(sv4[:, :, :, 64:65], 1.0),
                       writes=["svones"])
                for t in range(NT):
                    pb = p_s[t % 2]
                    pt = f"ps:{t % 2}"
                    for c in range(NCH):
                        sq.add("tensor", lambda e, wb=wb, c=c, t=t, pb=pb: e.matmul(
                            pb[:, 0:128], shT[:, c, 128 * t:128 * (t + 1)],
                            swv[:, wb, c, :], start=(c == 0), stop=False),
                            reads=[f"wv:{wb}", f"hT:{c}:{t // 4}"], writes=[pt])
                    sq.add("tensor", lambda e, d=d, pb=pb: e.matmul(
                        pb[:, 0:128], ones1[:], sbvb[:, 128 * d:128 * (d + 1)],
                        start=False, stop=True),
                        reads=["bvb", "ones1"], writes=[pt])
                    sq.add("vector", lambda e, t=t, pb=pb: e.tensor_copy(
                        sv4[:, t, :, 0:64],
                        pb[:, 0:128].rearrange("p (g c) -> p g c", c=64)),
                        reads=[pt], writes=[f"sv:{t}"])

                if d + 2 < NP:
                    wload(d + 2)

                for hb in range(2):
                    o = 64 * hb
                    h = 2 * d + hb
                    for qc in range(NQC):
                        nkt = 4 * qc + 4

                        def emit_qk(kt, qc=qc, o=o):
                            qoff = max(0, 128 * kt - 512 * qc)
                            pb = p_s[kt % 4]
                            sq.add("tensor", lambda e, kt=kt, qoff=qoff, pb=pb,
                                   o=o, qc=qc: e.matmul(
                                       pb[:, qoff:512],
                                       skt[o:o + 64, 128 * kt:128 * (kt + 1)],
                                       sqt[o:o + 64, 512 * qc + qoff:512 * (qc + 1)],
                                       start=True, stop=True, tile_position=(o, 0)),
                                   reads=["sqt", "skt"], writes=[f"ps:{kt % 4}"])

                        def emit_sm(kt, qc=qc, h=h):
                            qoff = max(0, 128 * kt - 512 * qc)
                            pb = p_s[kt % 4]
                            sq.add("scalar", lambda e, kt=kt, qoff=qoff, pb=pb: e.activation(
                                sPt[:, kt % 4, qoff:512], pb[:, qoff:512],
                                AF.Exp, bias=EXPB, scale=F1),
                                reads=[f"ps:{kt % 4}"], writes=[f"pt:{kt % 4}"])
                            j = kt - 4 * qc
                            if j >= 0:
                                w = min(240, 512 - 128 * j)
                                sq.add("vector", lambda e, kt=kt, j=j, w=w, h=h: e.tensor_mul(
                                    sPt[:, kt % 4, 128 * j:128 * j + w],
                                    sPt[:, kt % 4, 128 * j:128 * j + w],
                                    sE[:, h, 0:w]),
                                    reads=[f"E:{h}"], writes=[f"pt:{kt % 4}"])
                            elif j == -1:
                                sq.add("vector", lambda e, kt=kt, h=h: e.tensor_mul(
                                    sPt[:, kt % 4, 0:112], sPt[:, kt % 4, 0:112],
                                    sE[:, h, 128:240]),
                                    reads=[f"E:{h}"], writes=[f"pt:{kt % 4}"])

                        def emit_pv(kt, qc=qc, hb=hb, nkt=nkt):
                            qoff = max(0, 128 * kt - 512 * qc)
                            sq.add("tensor", lambda e, kt=kt, qoff=qoff, hb=hb,
                                   qc=qc, nkt=nkt: e.matmul(
                                       p_oacc[:, 512 * qc + qoff:512 * (qc + 1)],
                                       sv[:, kt, 65 * hb:65 * hb + 65],
                                       sPt[:, kt % 4, qoff:512],
                                       start=(kt == 0), stop=(kt == nkt - 1)),
                                   reads=[f"pt:{kt % 4}", f"sv:{kt}", "svones"],
                                   writes=["oacc"])

                        L = 3
                        for kt in range(min(L, nkt)):
                            emit_qk(kt)
                        for kt in range(nkt):
                            if kt + L < nkt:
                                emit_qk(kt + L)
                            emit_sm(kt)
                            emit_pv(kt)

                        if qc == 0 and epi["a"] is not None:
                            epi_A()
                            epi["b"] = epi["a"]
                            epi["a"] = None

                    if epi["b"] is not None:
                        epi_B(*epi["b"])
                        epi["b"] = None

                    for qc in range(NQC):
                        sq.add("vector", lambda e, qc=qc, o=o, d=d: e.tensor_copy(
                            soT[o:o + 64, d, 512 * qc:512 * (qc + 1)],
                            p_oacc[0:64, 512 * qc:512 * (qc + 1)]),
                            reads=["oacc"], writes=["soT"])
                        sq.add("scalar", lambda e, qc=qc: e.activation(
                            srows1[0:1, qc, :],
                            p_oacc[64:65, 512 * qc:512 * (qc + 1)],
                            AF.Identity, bias=F0, scale=F1),
                            reads=["oacc"], writes=["sr1"])
                    sq.add("sync", lambda e: e.dma_start(
                        out=dsum[0:1, :],
                        in_=srows1[:].rearrange("o q j -> o (q j)")),
                        reads=["sr1"], writes=["dsum"], dma=True)
                    epi["a"] = (d, o)

            if epi["a"] is not None:
                epi_A()
                epi["b"] = epi["a"]
                epi["a"] = None
            if epi["b"] is not None:
                epi_B(*epi["b"])
                epi["b"] = None

            for t in range(NT):
                for half in range(2):
                    pb = p_s[(2 * t + half) % 2]
                    pt = f"ps:{(2 * t + half) % 2}"
                    for c in range(NCH):
                        sq.add("tensor", lambda e, t=t, half=half, c=c, pb=pb: e.matmul(
                            pb[:], soT[:, c, 128 * t:128 * (t + 1)],
                            swo[:, c, 512 * half:512 * (half + 1)],
                            start=(c == 0), stop=False),
                            reads=["soT", "swo"], writes=[pt])
                    sq.add("tensor", lambda e, half=half, pb=pb: e.matmul(
                        pb[:], ones1[:], sbob[:, 512 * half:512 * (half + 1)],
                        start=False, stop=True),
                        reads=["bob", "ones1"], writes=[pt])
                    sq.add("vector", lambda e, half=half, pb=pb, pt=pt: e.scalar_tensor_tensor(
                        p_s[2][:], pb[:], 1.0, gat[:, 512 * half:512 * (half + 1)],
                        ALU.mult, ALU.mult),
                        reads=[pt, "gat"], writes=["ps:2"])
                    sq.add("vector", lambda e, t=t, half=half: e.tensor_add(
                        sx[:, t, 512 * half:512 * (half + 1)],
                        sx[:, t, 512 * half:512 * (half + 1)], p_s[2][:]),
                        reads=["ps:2"], writes=[f"x:{t}"])

        sq.barrier()

        with ExitStack() as a3:
            sq.enabled = (upto >= 3)

            def sb3(name, shape, dtp):
                return a3.enter_context(nc.sbuf_tensor(name, shape, dtp, side="right"))

            def ps3(name, shape):
                return a3.enter_context(nc.psum_tensor(name, shape, F32))

            spk = sb3("spk", [128, NCH, 256], I32)
            swf = sb3("swf", [128, NCH, D], BF16)
            sy3 = sb3("sy3", [128, 2, D], BF16)
            sh2 = sb3("sh2", [128, 2, D], BF16)
            st32 = sb3("st32", [128, D], F32)
            sqi = sb3("sqi", [128, D], I32)
            sei = sb3("sei", [128, 256], I32)
            gbw = sb3("gbw", [128, D], BF16)
            gbiast = sb3("gbiast", [128, D], BF16)
            a2t = sb3("a2t", [128, D], BF16)
            b2t = sb3("b2t", [128, D], BF16)
            srowb3 = sb3("srowb3", [1, D], BF16)
            p_f = [ps3(f"p_f{i}", [128, 512]) for i in range(2)]
            p_g3 = ps3("p_g3", [128, 512])
            p_tr3 = [a3.enter_context(nc.psum_tensor(f"p_tr3{i}", [128, 128], BF16))
                     for i in range(2)]

            def rebuild(dst, tag, slot, plus1=False):
                sq.add("sync", lambda e, slot=slot: e.dma_start(
                    out=srowb3[:], in_=drow2[:, D * slot:D * (slot + 1)]),
                    reads=[f"dr2:{slot}"], writes=["srowb3"], dma=True)
                for half in range(2):
                    sq.add("tensor", lambda e, half=half: e.matmul(
                        p_g3[:], ones1[:], srowb3[:, 512 * half:512 * (half + 1)],
                        start=True, stop=True),
                        reads=["srowb3", "ones1"], writes=["pg3"])
                    bias = F1 if plus1 else F0
                    sq.add("scalar", lambda e, half=half, dst=dst, bias=bias: e.activation(
                        dst[:, 512 * half:512 * (half + 1)], p_g3[:], AF.Identity,
                        bias=bias, scale=F1),
                        reads=["pg3"], writes=[tag])

            rebuild(gbw, "gbw", 0)
            rebuild(gbiast, "gbiast", 1)
            rebuild(a2t, "a2t", 2, plus1=True)
            rebuild(b2t, "b2t", 3)

            for c in range(NCH):
                sq.add("sync", lambda e, c=c: e.dma_start(
                    out=spk[:, c, :], in_=pkt[128 * c:128 * (c + 1), :]),
                    writes=["spk"], dma=True)
            for c in range(NCH):
                for s in range(4):
                    sq.add("vector", lambda e, c=c, s=s: e.tensor_scalar(
                        sei[:], spk[:, c, :], 2 * s, 3,
                        ALU.logical_shift_right, ALU.bitwise_and),
                        reads=["spk"], writes=["sei"])
                    sq.add("vector", lambda e, c=c, s=s: e.tensor_scalar(
                        swf[:, c, 256 * s:256 * (s + 1)], sei[:], 1, None,
                        ALU.subtract),
                        reads=["sei"], writes=[f"swf:{c}"])
                sq.add("vector", lambda e, c=c: e.tensor_mul(
                    swf[:, c, :], swf[:, c, :], gbw[:]),
                    reads=["gbw"], writes=[f"swf:{c}"])

            for t in range(NT):
                ln_stats(t)
            ln_batch()
            for t in range(NT):
                b = t % 2
                ln_apply(t, sy3[:, b, :], f"sy:{b}")
                sq.add("vector", lambda e, b=b: e.tensor_mul(
                    sh2[:, b, :], sy3[:, b, :], a2t[:]),
                    reads=[f"sy:{b}", "a2t"], writes=[f"sh2:{b}"])
                sq.add("vector", lambda e, b=b: e.tensor_add(
                    sh2[:, b, :], sh2[:, b, :], b2t[:]),
                    reads=["b2t", f"sh2:{b}"], writes=[f"sh2:{b}"])
                sq.add("vector", lambda e, b=b: e.reduce_max(
                    stmp1[:], sh2[:, b, :], AXX, apply_absolute_value=True),
                    reads=[f"sh2:{b}"], writes=["t1"])
                sq.add("vector", lambda e: e.tensor_scalar_max(stmp1[:], stmp1[:], 1e-5),
                       reads=["t1"], writes=["t1"])
                sq.add("vector", lambda e, t=t: e.tensor_scalar(
                    skap[:, t:t + 1], stmp1[:], s_swb[:], 1.0 / 127.0,
                    ALU.mult, ALU.mult),
                    reads=["t1", "swb"], writes=["kap"])
                sq.add("vector", lambda e: e.reciprocal(stmp1b[:], stmp1[:]),
                       reads=["t1"], writes=["t1b"])
                sq.add("vector", lambda e, t=t: e.tensor_scalar_mul(
                    ssc[:, t:t + 1], stmp1b[:], 127.0),
                    reads=["t1b"], writes=["qsc"])
                sq.add("scalar", lambda e, t=t, b=b: e.activation(
                    st32[:], sh2[:, b, :], AF.Identity, scale=ssc[:, t:t + 1],
                    bias=F0),
                    reads=[f"sh2:{b}", "qsc"], writes=["st32"])
                sq.add("vector", lambda e: e.tensor_copy(sqi[:], st32[:]),
                       reads=["st32"], writes=["sqi"])
                sq.add("vector", lambda e, b=b: e.tensor_copy(sy3[:, b, :], sqi[:]),
                       reads=["sqi"], writes=[f"sy:{b}"])
                for c in range(NCH):
                    pp = c % 2
                    sq.add("tensor", lambda e, b=b, c=c, pp=pp: e.transpose(
                        p_tr3[pp][:], sy3[:, b, 128 * c:128 * (c + 1)], sid128[:]),
                        reads=[f"sy:{b}", "sid"], writes=[f"ptr:{pp}"])
                    sq.add("vector", lambda e, t=t, c=c, pp=pp: e.tensor_copy(
                        shT[:, c, 128 * t:128 * (t + 1)], p_tr3[pp][:]),
                        reads=[f"ptr:{pp}"], writes=[f"hT:{c}:{t // 4}"])

            for t in range(NT):
                for half in range(2):
                    pb = p_f[(2 * t + half) % 2]
                    pt = f"pf:{(2 * t + half) % 2}"
                    for c in range(NCH):
                        sq.add("tensor", lambda e, t=t, half=half, c=c, pb=pb: e.matmul(
                            pb[:], shT[:, c, 128 * t:128 * (t + 1)],
                            swf[:, c, 512 * half:512 * (half + 1)],
                            start=(c == 0), stop=(c == NCH - 1)),
                            reads=[f"hT:{c}:{t // 4}", f"swf:{c}"], writes=[pt]),
                    sq.add("vector", lambda e, t=t, half=half, pb=pb, pt=pt: e.scalar_tensor_tensor(
                        p_g3[:], pb[:], skap[:, t:t + 1],
                        gbiast[:, 512 * half:512 * (half + 1)],
                        ALU.mult, ALU.add),
                        reads=[pt, "kap", "gbiast"], writes=["pg3"])
                    sq.add("vector", lambda e, t=t, half=half: e.tensor_add(
                        sx[:, t, 512 * half:512 * (half + 1)],
                        sx[:, t, 512 * half:512 * (half + 1)], p_g3[:]),
                        reads=["pg3"], writes=[f"x:{t}"])

                sq.add("sync", lambda e, t=t: e.dma_start(
                    out=out[128 * t:128 * (t + 1), :], in_=sx[:, t, :]),
                    reads=[f"x:{t}"], writes=[f"out:{t}"], dma=True)
            sq.add("vector", lambda e: e.engine_nop(),
                   reads=[f"out:{t}" for t in range(NT)], writes=["done"])

        block = ctx.enter_context(nc.Block())
        sq.build(block, sems, serial=serial)
    return nc


_CACHED = {}


def _get_nc():
    if "nc" not in _CACHED:
        _CACHED["nc"] = build_nc()
    return _CACHED["nc"]


def kernel(L0_attn_x, L0_attn_Wq, L0_attn_bq, L0_attn_Wk, L0_attn_bk,
           L0_attn_Wv, L0_attn_bv, L0_attn_Wo, L0_attn_bo, L0_attn_rel_emb,
           L0_ffn_packed_weight, L0_ffn_weight_scale, L0_ffn_bias, L0_c,
           L0_wmod):
    x = np.ascontiguousarray(np.asarray(L0_attn_x, np.float32))
    wqT = np.ascontiguousarray(np.asarray(L0_attn_Wq, np.float32).T)
    wkT = np.ascontiguousarray(np.asarray(L0_attn_Wk, np.float32).T)
    wvT = np.ascontiguousarray(np.asarray(L0_attn_Wv, np.float32).T)
    woT = np.ascontiguousarray(np.asarray(L0_attn_Wo, np.float32).T)
    wmv = np.ascontiguousarray(np.asarray(L0_wmod, np.float32))
    pkT = np.ascontiguousarray(
        np.asarray(L0_ffn_packed_weight, np.int32).reshape(256, 1024).T)
    relv = np.ascontiguousarray(np.asarray(L0_attn_rel_emb, np.float32))
    ohv = _oh_const()
    c = np.asarray(L0_c, np.float32)
    bqcv = np.ascontiguousarray(np.asarray(L0_attn_bq, np.float32).reshape(8, 128).T)
    bkcv = np.ascontiguousarray(np.asarray(L0_attn_bk, np.float32).reshape(8, 128).T)
    bvr = np.ascontiguousarray(np.asarray(L0_attn_bv, np.float32).reshape(1, D))
    borv = np.ascontiguousarray(np.asarray(L0_attn_bo, np.float32).reshape(1, D))
    fbrv = np.ascontiguousarray(np.asarray(L0_ffn_bias, np.float32).reshape(1, D))
    wscv = np.asarray(L0_ffn_weight_scale, np.float32).reshape(1, 1)
    eye16 = np.eye(16, dtype=np.float32)
    eye128 = np.eye(128, dtype=ml_dtypes.bfloat16)

    nc = _get_nc()
    in_maps = []
    for b in range(8):
        in_maps.append({
            "x": x[b], "wq": wqT, "wk": wkT, "wv": wvT, "wo": woT,
            "wm": wmv, "pk": pkT,
            "ccol": np.ascontiguousarray(c[b].reshape(8, 128).T),
            "bqc": bqcv, "bkc": bkcv, "bvr": bvr, "bor": borv, "fbr": fbrv,
            "rel": relv, "oh": ohv, "wsc": wscv, "idf": eye16,
            "eye128": eye128,
        })
    res = run_bass_kernel_spmd(nc, in_maps, core_ids=list(range(8)))
    return np.stack([r["out"] for r in res.results], axis=0)
